# version 05
# baseline (speedup 1.0000x reference)
import contextlib
import numpy as np

import concourse.bass as bass
import concourse.mybir as mybir
from concourse.bass_utils import run_bass_kernel_spmd

F32 = mybir.dt.float32
BF16 = mybir.dt.bfloat16
F16 = mybir.dt.float16
AF = mybir.ActivationFunctionType
ALU = mybir.AluOpType

D = 2048
T = 1024
ECN = 16
TW = 512
TCN = T // TW
INTER = 5632
ICN = INTER // 128
ICNP = 48
QN = 4
ICQ = ICN // QN
DNC = ECN * QN
SCALE = float(128.0 ** -0.5)
NCORES = 8
EPS = 1e-5

ENGINES = ("sync", "scalar", "vector", "tensor", "gpsimd")


class Prog:

    def __init__(self, nc, es, n_dma_sems=24):
        self.nc = nc
        self.ops = {e: [] for e in ENGINES}
        self.esem = {e: es.enter_context(nc.semaphore("es_" + e)) for e in ENGINES}
        self.ecount = {e: 0 for e in ENGINES}
        self.dsems = [es.enter_context(nc.semaphore("dm%d" % i)) for i in range(n_dma_sems)]
        self.duse = [0] * n_dma_sems
        self.dnext = 0
        self.gsems = [es.enter_context(nc.semaphore("gm%d" % i)) for i in range(10)]
        self.guse = [0] * 10
        self.gnext = 0
        self.last_w = {}
        self.readers = {}
        self.waited = {e: {} for e in ENGINES}
        self.seen_keys = set()
        self.graveyard = []
        self.n_inst = 0

    def _marks_for(self, reads, writes):
        marks = []
        for r in reads:
            m = self.last_w.get(r)
            if m is not None:
                marks.append(m)
        for w in writes:
            m = self.last_w.get(w)
            if m is not None:
                marks.append(m)
            for m2 in self.readers.get(w, {}).values():
                marks.append(m2)
            if w not in self.seen_keys:
                marks.extend(self.graveyard)
        return marks

    def _dedupe(self, eng, marks):
        out = []
        w = self.waited[eng]
        for (sem, val) in marks:
            k = id(sem)
            if w.get(k, 0) >= val:
                continue
            w[k] = val
            out.append((sem, val))
        return out

    def _commit(self, eng, mark, reads, writes):
        for wkey in writes:
            self.last_w[wkey] = mark
            self.readers[wkey] = {}
            self.seen_keys.add(wkey)
        for r in reads:
            d = self.readers.setdefault(r, {})
            k = id(mark[0])
            if k not in d or d[k][1] < mark[1]:
                d[k] = mark

    def close_scope(self, keys):
        for k in keys:
            m = self.last_w.get(k)
            if m is not None:
                self.graveyard.append(m)
            for m2 in self.readers.get(k, {}).values():
                self.graveyard.append(m2)
        best = {}
        for (sem, val) in self.graveyard:
            if id(sem) not in best or best[id(sem)][1] < val:
                best[id(sem)] = (sem, val)
        self.graveyard = list(best.values())

    def op(self, eng, fn, reads=(), writes=()):
        return self.macro(eng, [fn], reads=reads, writes=writes)

    def macro(self, eng, fns, reads=(), writes=()):
        reads = list(reads)
        writes = list(writes)
        waits = self._dedupe(eng, self._marks_for(reads, writes))
        self.ecount[eng] += 1
        mark = (self.esem[eng], self.ecount[eng])
        sem = self.esem[eng]
        self.n_inst += len(fns) + len(waits)

        def emit(E, fns=fns, waits=waits, sem=sem):
            for (s, v) in waits:
                E.wait_ge(s, v)
            inst = None
            for f in fns:
                inst = f(E)
            inst.then_inc(sem, 1)
        self.ops[eng].append(emit)
        self._commit(eng, mark, reads, writes)
        return mark

    def raw(self, eng, fn):
        self.ops[eng].append(fn)
        self.n_inst += 1

    def dma(self, eng, fn, reads=(), writes=()):
        reads = list(reads)
        writes = list(writes)
        if eng == "gpsimd":
            j = self.gnext
            self.gnext = (self.gnext + 1) % len(self.gsems)
            s = self.gsems[j]
            use = self.guse
        else:
            j = self.dnext
            self.dnext = (self.dnext + 1) % len(self.dsems)
            s = self.dsems[j]
            use = self.duse
        marks = self._marks_for(reads, writes)
        if use[j] > 0:
            marks.append((s, 16 * use[j]))
        waits = self._dedupe(eng, marks)
        use[j] += 1
        mark = (s, 16 * use[j])
        self.n_inst += 1 + len(waits)

        def emit(E, fn=fn, waits=waits, s=s):
            for (ss, v) in waits:
                E.wait_ge(ss, v)
            fn(E).then_inc(s, 16)
        self.ops[eng].append(emit)
        self._commit(eng, mark, reads, writes)
        return mark


def make_stream(p, demands, pool, pool_tag, rr, pf=3):
    info = {}
    state = {"emitted": 0}
    paths = ("sync", "scalar", "gpsimd")

    def emit_one(i):
        full, chunk, width, agkey = demands[i]
        sl = i % len(pool)
        kT = ("wT", pool_tag, sl)
        eng = paths[rr[0] % 3]
        rr[0] += 1
        p.dma(eng,
              lambda E, d=pool[sl], s=full, c=chunk, w=width:
              E.dma_start(out=d[:, 0:w], in_=s[c * 128:(c + 1) * 128, 0:w]),
              reads=[agkey], writes=[kT])
        info[i] = (sl, [kT])

    def get(i):
        hi = min(i + pf, len(demands) - 1)
        while state["emitted"] <= hi:
            emit_one(state["emitted"])
            state["emitted"] += 1
        return info[i]

    return get


def build_nc():
    nc = bass.Bass("TRN2", target_bir_lowering=False)

    dx = nc.dram_tensor("x", [T, D], F32, kind="ExternalInput")
    dwq_my = nc.dram_tensor("wq_my", [256, D], F32, kind="ExternalInput")
    dwk_my = nc.dram_tensor("wk_my", [256, D], F32, kind="ExternalInput")
    dwv_my = nc.dram_tensor("wv_my", [256, D], F32, kind="ExternalInput")
    dwo_my = nc.dram_tensor("wo_my", [256, D], F32, kind="ExternalInput")
    dgw_my = nc.dram_tensor("gw_my", [768, D], F32, kind="ExternalInput")
    duw_my = nc.dram_tensor("uw_my", [768, D], F32, kind="ExternalInput")
    ddw_my = nc.dram_tensor("dw_my", [1024, 1408], F32, kind="ExternalInput")
    dbq = nc.dram_tensor("bq", [128, 16], F32, kind="ExternalInput")
    dbk = nc.dram_tensor("bk", [128, 16], F32, kind="ExternalInput")
    dbv = nc.dram_tensor("bv", [128, 16], F32, kind="ExternalInput")
    dbo = nc.dram_tensor("bo", [128, 16], F32, kind="ExternalInput")
    dgs = nc.dram_tensor("gs", [128, 44], F32, kind="ExternalInput")
    dus = nc.dram_tensor("us", [128, 44], F32, kind="ExternalInput")
    dds = nc.dram_tensor("ds", [128, 16], F32, kind="ExternalInput")
    dsendm = nc.dram_tensor("sendm", [128, 16], F32, kind="ExternalInput")
    dusem = nc.dram_tensor("usem", [128, 16], F32, kind="ExternalInput")
    dident = nc.dram_tensor("ident", [128, 128], F32, kind="ExternalInput")
    dy = nc.dram_tensor("y", [T, D], F32, kind="ExternalOutput")
    dccin = nc.dram_tensor("ccin", [128, 16], F32)
    dccout = nc.dram_tensor("ccout", [128, 16], F32)

    shq = nc.dram_tensor("shq", [256, D], BF16)
    shk = nc.dram_tensor("shk", [256, D], BF16)
    shv = nc.dram_tensor("shv", [256, D], BF16)
    sho = nc.dram_tensor("sho", [256, D], F16)
    shg = nc.dram_tensor("shg", [768, D], BF16)
    shu = nc.dram_tensor("shu", [768, D], BF16)
    shd = nc.dram_tensor("shd", [1024, 1408], BF16)
    fq = nc.dram_tensor("fq", [ECN * 128, D], BF16)
    fk = nc.dram_tensor("fk", [ECN * 128, D], BF16)
    fv = nc.dram_tensor("fv", [ECN * 128, D], BF16)
    fo = nc.dram_tensor("fo", [ECN * 128, D], F16)
    fg = nc.dram_tensor("fg", [ICNP * 128, D], BF16)
    fu = nc.dram_tensor("fu", [ICNP * 128, D], BF16)
    fd = nc.dram_tensor("fd", [DNC * 128, 1408], BF16)

    es = contextlib.ExitStack()
    with es:
        p = Prog(nc, es)
        sb = nc.sbuf_tensor

        def alive(guard):
            return guard.__enter__(), guard

        s_bq = es.enter_context(sb("s_bq", [128, 16], F32))
        s_bk = es.enter_context(sb("s_bk", [128, 16], F32))
        s_bv = es.enter_context(sb("s_bv", [128, 16], F32))
        s_bo = es.enter_context(sb("s_bo", [128, 16], F32))
        s_gs = es.enter_context(sb("s_gs", [128, 44], F32))
        s_us = es.enter_context(sb("s_us", [128, 44], F32))
        s_ds = es.enter_context(sb("s_ds", [128, 16], F32))
        s_sendm = es.enter_context(sb("s_sendm", [128, 16], F32))
        s_usem = es.enter_context(sb("s_usem", [128, 16], F32))
        s_ls = es.enter_context(sb("s_ls", [128, 16], F32))
        s_ls2 = es.enter_context(sb("s_ls2", [128, 2], F32))
        s_send = es.enter_context(sb("s_send", [128, 16], F32))
        s_crecv = es.enter_context(sb("s_crecv", [128, 16], F32))
        s_carry = es.enter_context(sb("s_carry", [128, 16], F32))
        s_stat = es.enter_context(sb("s_stat", [128, 8], F32))
        s_eps = es.enter_context(sb("s_eps", [128, 1], F32))
        s_ones = es.enter_context(sb("s_ones", [128, T], F32))
        s_xn = es.enter_context(sb("s_xn", [128, ECN * T], BF16))
        s_ctx, g_ctx = alive(sb("ctx", [128, ECN * T], F16))

        pp = [es.enter_context(nc.psum_tensor("pp%d" % i, [128, TW], F32))
              for i in range(4)]
        pt = [es.enter_context(nc.psum_tensor("pt%d" % i, [128, TW], F32))
              for i in range(2)]

        ppi = [0]

        def next_pp():
            i = ppi[0] % 4
            ppi[0] += 1
            return i

        def mm_group(pslot, pairs, reads):
            fns = []
            n = len(pairs)
            for k, (st, mv) in enumerate(pairs):
                fns.append(lambda E, st=st, mv=mv, s=(k == 0), e=(k == n - 1):
                           E.matmul(pp[pslot][:], st, mv, start=s, stop=e))
            return p.macro("tensor", fns, reads=reads, writes=[("pp", pslot)])

        for key, dsrc, sdst in (
            (("bq",), dbq, s_bq), (("bk",), dbk, s_bk), (("bv",), dbv, s_bv),
            (("bo",), dbo, s_bo), (("gs",), dgs, s_gs), (("us",), dus, s_us),
            (("ds",), dds, s_ds), (("sendm",), dsendm, s_sendm),
            (("usem",), dusem, s_usem),
        ):
            p.dma("sync", lambda E, d=sdst, s=dsrc: E.dma_start(out=d[:], in_=s[:, :]),
                  reads=[], writes=[key])
        p.op("gpsimd", lambda E: E.memset(s_ones[:], 1.0), reads=[], writes=[("ones",)])
        p.op("gpsimd", lambda E: E.memset(s_eps[:], EPS), reads=[], writes=[("eps",)])

        s_wT, g_wT = zip(*[alive(sb("wTa%d" % i, [128, 2048], BF16)) for i in range(5)])
        s_t1, g_t1 = alive(sb("t1", [128, TW], F32))
        s_t2, g_t2 = alive(sb("t2", [128, TW], F32))
        s_t3, g_t3 = alive(sb("t3", [128, TW], F32))
        s_kvt, g_kvt = alive(sb("kvt", [128, T], F32))

        s_xl, g_xl = zip(*[alive(sb("xl%d" % i, [128, D], F32)) for i in range(2)])
        s_xnt, g_xnt = zip(*[alive(sb("xnt%d" % i, [128, D], BF16)) for i in range(2)])
        s_sq, g_sq = alive(sb("sq", [128, D], F32))
        s_pwc, g_pwc = zip(*[alive(sb("pwc%d" % i, [128, D], BF16)) for i in range(2)])
        s_pwT, g_pwT = zip(*[alive(sb("pwT%d" % i, [128, D], BF16)) for i in range(2)])
        s_pwc6, g_pwc6 = alive(sb("pwc6", [128, D], F16))
        s_pwT6, g_pwT6 = alive(sb("pwT6", [128, D], F16))

        prep_rr = [0]

        def prep_matrix(tag, src_my, shard, full, cpc, width, f16=False):
            for j in range(cpc):
                if f16:
                    wc, wt = s_pwc6, s_pwT6
                    kc, kt = ("pwc6",), ("pwT6",)
                else:
                    i2 = prep_rr[0] % 2
                    prep_rr[0] += 1
                    wc, wt = s_pwc[i2], s_pwT[i2]
                    kc, kt = ("pwc", i2), ("pwT", i2)
                p.dma("gpsimd",
                      lambda E, d=wc, s=src_my, j=j, w=width:
                      E.dma_start(out=d[:, 0:w], in_=s[j * 128:(j + 1) * 128, 0:w]),
                      reads=[], writes=[kc])
                p.dma("scalar",
                      lambda E, o=wt, ii=wc, w=width:
                      E.dma_start_transpose(
                          out=o[:, 0:w].rearrange("q (c e) -> q c e", c=w // 128),
                          in_=ii[:, 0:w]),
                      reads=[kc], writes=[kt])
                p.dma("sync",
                      lambda E, d=shard, s=wt, j=j, w=width:
                      E.dma_start(out=d[j * 128:(j + 1) * 128, 0:w], in_=s[:, 0:w]),
                      reads=[kt], writes=[("sh", tag, j)])
            p.op("gpsimd",
                 lambda E, sh=shard, fl=full:
                 E.collective_compute(
                     "AllGather", ALU.bypass,
                     replica_groups=[list(range(NCORES))],
                     ins=[sh.ap().opt()], outs=[fl.ap().opt()]),
                 reads=[("sh", tag, j) for j in range(cpc)],
                 writes=[("agf", tag)])

        prep_matrix("k", dwk_my, shk, fk, 2, D)
        prep_matrix("v", dwv_my, shv, fv, 2, D)

        for tt in range(8):
            sl = tt % 2
            kx = ("xl", sl)
            p.dma("sync", lambda E, d=s_xl[sl], tt=tt:
                  E.dma_start(out=d[:], in_=dx[tt * 128:(tt + 1) * 128, :]),
                  reads=[], writes=[kx])
            p.op("vector", lambda E, ii=s_xl[sl]:
                 E.reduce_sum(s_stat[:, 0:1], ii[:], axis=mybir.AxisListType.X),
                 reads=[kx], writes=[("st", 0)])
            p.op("scalar", lambda E, ii=s_xl[sl]:
                 E.activation(s_sq[:], ii[:], AF.Square, accum_out=s_stat[:, 1:2]),
                 reads=[kx], writes=[("sq",), ("st", 1)])
            p.op("vector", lambda E:
                 E.tensor_scalar_mul(s_stat[:, 2:3], s_stat[:, 0:1], -1.0 / D),
                 reads=[("st", 0)], writes=[("st", 2)])
            p.op("vector", lambda E:
                 E.tensor_mul(s_stat[:, 3:4], s_stat[:, 2:3], s_stat[:, 2:3]),
                 reads=[("st", 2)], writes=[("st", 3)])
            p.op("vector", lambda E:
                 E.tensor_scalar_mul(s_stat[:, 4:5], s_stat[:, 1:2], 1.0 / D),
                 reads=[("st", 1)], writes=[("st", 4)])
            p.op("vector", lambda E:
                 E.tensor_sub(s_stat[:, 5:6], s_stat[:, 4:5], s_stat[:, 3:4]),
                 reads=[("st", 3), ("st", 4)], writes=[("st", 5)])
            p.op("scalar", lambda E:
                 E.activation(s_stat[:, 6:7], s_stat[:, 5:6], AF.Sqrt, bias=s_eps[:]),
                 reads=[("st", 5), ("eps",)], writes=[("st", 6)])
            p.op("vector", lambda E:
                 E.reciprocal(s_stat[:, 7:8], s_stat[:, 6:7]),
                 reads=[("st", 6)], writes=[("st", 7)])
            kxn = ("xnt", sl)
            p.op("vector", lambda E, ii=s_xl[sl], o=s_xnt[sl]:
                 E.tensor_scalar(o[:], ii[:], s_stat[:, 2:3], s_stat[:, 7:8],
                                 ALU.add, ALU.mult),
                 reads=[kx, ("st", 2), ("st", 7)], writes=[kxn])
            out_ap = (s_xn[:].rearrange("q (c t) -> q c t", c=ECN)
                      [:, :, tt * 128:(tt + 1) * 128])
            p.dma("scalar", lambda E, o=out_ap, ii=s_xnt[sl]:
                  E.dma_start_transpose(out=o, in_=ii[:]),
                  reads=[kxn], writes=[("xn", tt)])
            if tt == 0:
                prep_matrix("q", dwq_my, shq, fq, 2, D)
            elif tt == 1:
                prep_matrix("o", dwo_my, sho, fo, 2, D, f16=True)
            elif tt == 2:
                prep_matrix("g", dgw_my, shg, fg, 6, D)
            elif tt == 4:
                prep_matrix("u", duw_my, shu, fu, 6, D)
            elif tt == 6:
                prep_matrix("d", ddw_my, shd, fd, 8, 1408)

        p.close_scope([("xl", 0), ("xl", 1), ("xnt", 0), ("xnt", 1), ("sq",),
                       ("pwc", 0), ("pwc", 1), ("pwT", 0), ("pwT", 1),
                       ("pwc6",), ("pwT6",)])
        g_pwT6.__exit__(None, None, None)
        g_pwc6.__exit__(None, None, None)
        for g in reversed(list(g_pwT)):
            g.__exit__(None, None, None)
        for g in reversed(list(g_pwc)):
            g.__exit__(None, None, None)
        g_sq.__exit__(None, None, None)
        for g in reversed(list(g_xnt)):
            g.__exit__(None, None, None)
        for g in reversed(list(g_xl)):
            g.__exit__(None, None, None)

        def xn_reads(tc):
            return [("xn", tt) for tt in range(tc * 4, tc * 4 + 4)]

        def mov_xn(dc, tc):
            return s_xn[:, dc * T + tc * TW: dc * T + tc * TW + TW]

        s_pkv, g_pkv = alive(sb("pkv", [128, ECN * T], F32))
        s_phiqf, g_phiqf = alive(sb("phiqf", [128, 8 * T], F32))

        def phi_evac(pslot, bias, out_ap, out_key):
            kp = ("pp", pslot)
            p.op("vector", lambda E, b=bias, pslot=pslot:
                 E.tensor_scalar(s_t1[:], pp[pslot][:], b, 0.0, ALU.add, ALU.min),
                 reads=[kp], writes=[("t1",)])
            p.op("scalar", lambda E:
                 E.activation(s_t2[:], s_t1[:], AF.Exp),
                 reads=[("t1",)], writes=[("t2",)])
            p.op("vector", lambda E, b=bias, pslot=pslot:
                 E.tensor_scalar(s_t3[:], pp[pslot][:], b, 0.0, ALU.add, ALU.max),
                 reads=[kp], writes=[("t3",)])
            p.op("vector", lambda E, o=out_ap:
                 E.tensor_add(o, s_t3[:], s_t2[:]),
                 reads=[("t2",), ("t3",)], writes=[out_key])

        stream_rr = [0]

        kv_demands = []
        for ec in range(ECN):
            kv_demands.append((fk, ec, D, ("agf", "k")))
            kv_demands.append((fv, ec, D, ("agf", "v")))
        kv_get = make_stream(p, kv_demands, s_wT, "a", stream_rr, pf=3)

        for ec in range(ECN):
            slk, rk = kv_get(2 * ec)
            slv, rv = kv_get(2 * ec + 1)
            for tc in range(TCN):
                ps = next_pp()
                mm_group(ps, [(s_wT[slk][:, dc * 128:(dc + 1) * 128], mov_xn(dc, tc))
                              for dc in range(ECN)], reads=rk + xn_reads(tc))
                phi_evac(ps, s_bk[:, ec:ec + 1], s_t1[:], ("t1",))
                ps2 = next_pp()
                mm_group(ps2, [(s_wT[slv][:, dc * 128:(dc + 1) * 128], mov_xn(dc, tc))
                               for dc in range(ECN)], reads=rv + xn_reads(tc))
                p.op("vector", lambda E, b=s_bv[:, ec:ec + 1], ps2=ps2:
                     E.tensor_scalar(s_t3[:], pp[ps2][:], b, None, ALU.add),
                     reads=[("pp", ps2)], writes=[("t3",)])
                p.op("vector", lambda E, ec=ec, tc=tc:
                     E.scalar_tensor_tensor(
                         s_pkv[:, ec * T + tc * TW: ec * T + (tc + 1) * TW],
                         s_t1[:], 1.0, s_t3[:], ALU.mult, ALU.mult,
                         accum_out=s_ls2[:, tc:tc + 1]),
                     reads=[("t1",), ("t3",)],
                     writes=[("pkv", ec, tc), ("ls2", tc)])
            p.op("vector", lambda E, ec=ec:
                 E.tensor_add(s_ls[:, ec:ec + 1], s_ls2[:, 0:1], s_ls2[:, 1:2]),
                 reads=[("ls2", 0), ("ls2", 1)], writes=[("ls", ec)])

        p.op("vector", lambda E:
             E.tensor_mul(s_send[:], s_ls[:], s_sendm[:]),
             reads=[("sendm",)] + [("ls", ec) for ec in range(ECN)],
             writes=[("send",)])

        q_demands = [(fq, ec, D, ("agf", "q")) for ec in range(ECN)]
        q_get = make_stream(p, q_demands, s_wT, "a", stream_rr, pf=3)
        q_get(0)
        q_get(1)

        p.dma("gpsimd", lambda E: E.dma_start(out=dccin[:, :], in_=s_send[:]),
              reads=[("send",)], writes=[("ccin",)])
        p.op("gpsimd", lambda E:
             E.collective_compute(
                 "AllReduce", ALU.add,
                 replica_groups=[[0, 1], [2, 3], [4, 5], [6, 7]],
                 ins=[dccin.ap().opt()], outs=[dccout.ap().opt()]),
             reads=[("ccin",)], writes=[("ccout",)])
        p.dma("gpsimd", lambda E: E.dma_start(out=s_crecv[:], in_=dccout[:, :]),
              reads=[("ccout",)], writes=[("crecv",)])
        p.op("vector", lambda E:
             E.tensor_mul(s_carry[:], s_crecv[:], s_usem[:]),
             reads=[("crecv",), ("usem",)], writes=[("carry",)])

        for half in range(2):
            for e8 in range(8):
                ec = half * 8 + e8
                sl, rk = q_get(ec)
                for tc in range(TCN):
                    ps = next_pp()
                    mm_group(ps, [(s_wT[sl][:, dc * 128:(dc + 1) * 128], mov_xn(dc, tc))
                                  for dc in range(ECN)], reads=rk + xn_reads(tc))
                    phi_evac(ps, s_bq[:, ec:ec + 1],
                             s_phiqf[:, e8 * T + tc * TW: e8 * T + (tc + 1) * TW],
                             ("phiqf", e8, tc))
            for e8 in range(8):
                ec = half * 8 + e8
                p.op("vector", lambda E, ec=ec:
                     E.tensor_tensor_scan(
                         s_kvt[:], s_ones[:],
                         s_pkv[:, ec * T:(ec + 1) * T], s_carry[:, ec:ec + 1],
                         ALU.mult, ALU.add),
                     reads=[("ones",), ("carry",), ("pkv", ec, 0), ("pkv", ec, 1)],
                     writes=[("kvt",)])
                p.op("vector", lambda E, ec=ec, e8=e8:
                     E.scalar_tensor_tensor(
                         s_ctx[:, ec * T:(ec + 1) * T],
                         s_phiqf[:, e8 * T:(e8 + 1) * T], SCALE, s_kvt[:],
                         ALU.mult, ALU.mult),
                     reads=[("kvt",), ("phiqf", e8, 0), ("phiqf", e8, 1)],
                     writes=[("ctx", ec)])

        p.close_scope(
            [("pkv", ec, tc) for ec in range(ECN) for tc in range(TCN)]
            + [("kvt",), ("t1",), ("t2",), ("t3",)]
            + [("phiqf", e8, tc) for e8 in range(8) for tc in range(TCN)]
            + [("wT", "a", sl) for sl in range(5)])
        g_phiqf.__exit__(None, None, None)
        g_pkv.__exit__(None, None, None)
        g_kvt.__exit__(None, None, None)
        g_t3.__exit__(None, None, None)
        g_t2.__exit__(None, None, None)
        g_t1.__exit__(None, None, None)
        for g in reversed(list(g_wT)):
            g.__exit__(None, None, None)

        s_acc, g_acc = alive(sb("s_acc", [128, ECN * T], F32))
        s_owT, g_owT = zip(*[alive(sb("owT%d" % i, [128, 2048], F16)) for i in range(4)])

        o_demands = [(fo, ec, D, ("agf", "o")) for ec in range(ECN)]
        o_get = make_stream(p, o_demands, s_owT, "o", stream_rr, pf=3)

        def mov_ctx(dc, tc):
            return s_ctx[:, dc * T + tc * TW: dc * T + tc * TW + TW]

        ctx_reads = [("ctx", ec) for ec in range(ECN)]
        for ec in range(ECN):
            sl, rk = o_get(ec)
            for tc in range(TCN):
                ps = next_pp()
                mm_group(ps, [(s_owT[sl][:, dc * 128:(dc + 1) * 128], mov_ctx(dc, tc))
                              for dc in range(ECN)], reads=rk + ctx_reads)
                p.op("vector", lambda E, ec=ec, tc=tc, ps=ps:
                     E.tensor_scalar(
                         s_acc[:, ec * T + tc * TW: ec * T + (tc + 1) * TW],
                         pp[ps][:], s_bo[:, ec:ec + 1], None, ALU.add),
                     reads=[("pp", ps), ("bo",)], writes=[("acc", ec, tc)])

        p.close_scope([("wT", "o", sl) for sl in range(4)])
        for g in reversed(list(g_owT)):
            g.__exit__(None, None, None)

        s_hdn, g_hdn = alive(sb("hdn", [128, ICQ * T], BF16))
        s_gu, g_gu = zip(*[alive(sb("gu%d" % i, [128, 2048], BF16)) for i in range(5)])
        s_dwT, g_dwT = zip(*[alive(sb("dwT%d" % i, [128, 1408], BF16)) for i in range(3)])
        s_sg, g_sg = alive(sb("sgt", [128, TW], F32))
        s_su, g_su = alive(sb("sut", [128, TW], F32))

        for q in range(QN):
            gu_demands = []
            for il in range(ICQ):
                ic = q * ICQ + il
                gu_demands.append((fg, ic, D, ("agf", "g")))
                gu_demands.append((fu, ic, D, ("agf", "u")))
            gu_get = make_stream(p, gu_demands, s_gu, ("gu", q), stream_rr, pf=3)

            for il in range(ICQ):
                ic = q * ICQ + il
                slg, rg = gu_get(2 * il)
                slu, ru = gu_get(2 * il + 1)
                for tc in range(TCN):
                    ps = next_pp()
                    mm_group(ps, [(s_gu[slg][:, dc * 128:(dc + 1) * 128], mov_xn(dc, tc))
                                  for dc in range(ECN)], reads=rg + xn_reads(tc))
                    p.op("scalar", lambda E, ic=ic, ps=ps:
                         E.activation(s_sg[:], pp[ps][:], AF.Silu,
                                      scale=s_gs[:, ic:ic + 1]),
                         reads=[("pp", ps), ("gs",)], writes=[("sg",)])
                    ps2 = next_pp()
                    mm_group(ps2, [(s_gu[slu][:, dc * 128:(dc + 1) * 128], mov_xn(dc, tc))
                                   for dc in range(ECN)], reads=ru + xn_reads(tc))
                    p.op("vector", lambda E, ic=ic, ps2=ps2:
                         E.tensor_scalar(s_su[:], pp[ps2][:], s_us[:, ic:ic + 1],
                                         None, ALU.mult),
                         reads=[("pp", ps2), ("us",)], writes=[("su",)])
                    p.op("vector", lambda E, il=il, tc=tc:
                         E.tensor_mul(
                             s_hdn[:, il * T + tc * TW: il * T + (tc + 1) * TW],
                             s_sg[:], s_su[:]),
                         reads=[("sg",), ("su",)], writes=[("hdn", il, tc)])

            dn_demands = [(fd, hc * QN + q, 1408, ("agf", "d")) for hc in range(ECN)]
            dn_get = make_stream(p, dn_demands, s_dwT, ("dn", q), stream_rr, pf=2)
            hdn_reads = [("hdn", il, tc) for il in range(ICQ) for tc in range(TCN)]
            for hc in range(ECN):
                sl, rk = dn_get(hc)
                for tc in range(TCN):
                    ps = next_pp()
                    mm_group(ps, [(s_dwT[sl][:, il * 128:(il + 1) * 128],
                                   s_hdn[:, il * T + tc * TW: il * T + tc * TW + TW])
                                  for il in range(ICQ)], reads=rk + hdn_reads)
                    p.op("vector", lambda E, hc=hc, tc=tc, ps=ps:
                         E.scalar_tensor_tensor(
                             s_acc[:, hc * T + tc * TW: hc * T + (tc + 1) * TW],
                             pp[ps][:], s_ds[:, hc:hc + 1],
                             s_acc[:, hc * T + tc * TW: hc * T + (tc + 1) * TW],
                             ALU.mult, ALU.add),
                         reads=[("pp", ps), ("ds",), ("acc", hc, tc)],
                         writes=[("acc", hc, tc)])

        ffn_keys = ([("hdn", il, tc) for il in range(ICQ) for tc in range(TCN)]
                    + [("sg",), ("su",)])
        for q in range(QN):
            ffn_keys += [("wT", ("gu", q), sl) for sl in range(5)]
            ffn_keys += [("wT", ("dn", q), sl) for sl in range(3)]
        p.close_scope(ffn_keys)
        g_su.__exit__(None, None, None)
        g_sg.__exit__(None, None, None)
        for g in reversed(list(g_dwT)):
            g.__exit__(None, None, None)
        for g in reversed(list(g_gu)):
            g.__exit__(None, None, None)
        g_hdn.__exit__(None, None, None)

        s_xt, g_xt = zip(*[alive(sb("xt%d" % i, [128, D], F32)) for i in range(2)])
        s_yt, g_yt = zip(*[alive(sb("yt%d" % i, [128, D], F32)) for i in range(2)])
        s_id, g_id = alive(sb("identity", [128, 128], F32))
        p.dma("sync", lambda E: E.dma_start(out=s_id[:], in_=dident[:, :]),
              reads=[], writes=[("id",)])
        out_marks = []

        for tt in range(8):
            sl = tt % 2
            kx = ("xt", sl)
            p.dma("sync", lambda E, d=s_xt[sl], tt=tt:
                  E.dma_start(out=d[:], in_=dx[tt * 128:(tt + 1) * 128, :]),
                  reads=[], writes=[kx])
            tc = tt // 4
            for dc in range(ECN):
                pslot = dc % 2
                p.op("tensor", lambda E, dc=dc, tt=tt, pslot=pslot:
                     E.transpose(pt[pslot][:, 0:128],
                                 s_acc[:, dc * T + tt * 128: dc * T + (tt + 1) * 128],
                                 s_id[:]),
                     reads=[("id",), ("acc", dc, tc)], writes=[("pt", pslot)])
                p.op("vector", lambda E, dc=dc, sl=sl, pslot=pslot:
                     E.tensor_add(s_yt[sl][:, dc * 128:(dc + 1) * 128],
                                  s_xt[sl][:, dc * 128:(dc + 1) * 128],
                                  pt[pslot][:, 0:128]),
                     reads=[kx, ("pt", pslot)], writes=[("yt", sl, dc)])
            out_marks.append(p.dma("sync", lambda E, sl=sl, tt=tt:
                      E.dma_start(out=dy[tt * 128:(tt + 1) * 128, :], in_=s_yt[sl][:]),
                      reads=[("yt", sl, dc) for dc in range(ECN)]))

        final = {}
        for (sem, val) in out_marks:
            if id(sem) not in final or final[id(sem)][1] < val:
                final[id(sem)] = (sem, val)
        for (sem, val) in final.values():
            p.raw("sync", lambda E, sem=sem, val=val: E.wait_ge(sem, val))

        g_id.__exit__(None, None, None)
        for g in reversed(list(g_yt)):
            g.__exit__(None, None, None)
        for g in reversed(list(g_xt)):
            g.__exit__(None, None, None)
        g_acc.__exit__(None, None, None)
        g_ctx.__exit__(None, None, None)

        with nc.Block() as block:
            block.sync(lambda E: [f(E) for f in p.ops["sync"]])
            block.scalar(lambda E: [f(E) for f in p.ops["scalar"]])
            block.vector(lambda E: [f(E) for f in p.ops["vector"]])
            block.tensor(lambda E: [f(E) for f in p.ops["tensor"]])
            block.gpsimd(lambda E: [f(E) for f in p.ops["gpsimd"]])

    return nc


_NC = None


def _get_nc():
    global _NC
    if _NC is None:
        _NC = build_nc()
    return _NC


def kernel(L0_attn_x, L0_attn_Wq, L0_attn_bq, L0_attn_Wk, L0_attn_bk,
           L0_attn_Wv, L0_attn_bv, L0_attn_Wo, L0_attn_bo,
           L0_ffn_gate_w, L0_ffn_up_w, L0_ffn_down_w,
           L0_ffn_gate_s, L0_ffn_up_s, L0_ffn_down_s):
    X = np.ascontiguousarray(np.asarray(L0_attn_x, dtype=np.float32))
    wq = np.asarray(L0_attn_Wq, dtype=np.float32)
    wk = np.asarray(L0_attn_Wk, dtype=np.float32)
    wv = np.asarray(L0_attn_Wv, dtype=np.float32)
    wo = np.asarray(L0_attn_Wo, dtype=np.float32)
    gw = np.asarray(L0_ffn_gate_w, dtype=np.float32)
    uw = np.asarray(L0_ffn_up_w, dtype=np.float32)
    dwn = np.asarray(L0_ffn_down_w, dtype=np.float32)

    gw_pad = np.zeros((ICNP * 128, D), np.float32)
    gw_pad[:INTER] = gw
    uw_pad = np.zeros((ICNP * 128, D), np.float32)
    uw_pad[:INTER] = uw
    dstack = (dwn.reshape(ECN, 128, QN, 1408)
              .transpose(0, 2, 1, 3).reshape(DNC * 128, 1408))

    def pcol(v, n):
        return np.ascontiguousarray(np.asarray(v, dtype=np.float32).reshape(n, 128).T)

    bq = pcol(L0_attn_bq, 16)
    bk = pcol(L0_attn_bk, 16)
    bv = pcol(L0_attn_bv, 16)
    bo = pcol(L0_attn_bo, 16)
    gs = pcol(L0_ffn_gate_s, 44)
    us = pcol(L0_ffn_up_s, 44)
    ds = pcol(L0_ffn_down_s, 16)
    ident = np.eye(128, dtype=np.float32)

    nc = _get_nc()
    in_maps = []
    for c in range(NCORES):
        b, h = c // 2, c % 2
        in_maps.append({
            "x": np.ascontiguousarray(X[b, h * T:(h + 1) * T]),
            "wq_my": np.ascontiguousarray(wq[c * 256:(c + 1) * 256]),
            "wk_my": np.ascontiguousarray(wk[c * 256:(c + 1) * 256]),
            "wv_my": np.ascontiguousarray(wv[c * 256:(c + 1) * 256]),
            "wo_my": np.ascontiguousarray(wo[c * 256:(c + 1) * 256]),
            "gw_my": np.ascontiguousarray(gw_pad[c * 768:(c + 1) * 768]),
            "uw_my": np.ascontiguousarray(uw_pad[c * 768:(c + 1) * 768]),
            "dw_my": np.ascontiguousarray(dstack[c * 1024:(c + 1) * 1024]),
            "bq": bq, "bk": bk, "bv": bv, "bo": bo,
            "gs": gs, "us": us, "ds": ds,
            "sendm": np.full((128, 16), 1.0 if h == 0 else 0.0, np.float32),
            "usem": np.full((128, 16), 1.0 if h == 1 else 0.0, np.float32),
            "ident": ident,
        })
    res = run_bass_kernel_spmd(nc, in_maps, core_ids=list(range(NCORES)))
    out = np.empty((4, 2048, D), np.float32)
    for c in range(NCORES):
        b, h = c // 2, c % 2
        out[b, h * T:(h + 1) * T] = res.results[c]["y"]
    return out


# version 06
# speedup vs baseline: 1.2555x; 1.2555x over previous
import contextlib
import numpy as np

import concourse.bass as bass
import concourse.mybir as mybir
from concourse.bass_utils import run_bass_kernel_spmd

F32 = mybir.dt.float32
BF16 = mybir.dt.bfloat16
F16 = mybir.dt.float16
AF = mybir.ActivationFunctionType
ALU = mybir.AluOpType

D = 2048
T = 1024
ECN = 16
TW = 512
TCN = T // TW
INTER = 5632
ICN = INTER // 128
QN = 4
ICQ = ICN // QN
SCALE = float(128.0 ** -0.5)
NCORES = 8
EPS = 1e-5

ENGINES = ("sync", "scalar", "vector", "tensor", "gpsimd")


class Prog:

    def __init__(self, nc, es, n_dma_sems=24):
        self.nc = nc
        self.ops = {e: [] for e in ENGINES}
        self.esem = {e: es.enter_context(nc.semaphore("es_" + e)) for e in ENGINES}
        self.ecount = {e: 0 for e in ENGINES}
        self.dsems = [es.enter_context(nc.semaphore("dm%d" % i)) for i in range(n_dma_sems)]
        self.duse = [0] * n_dma_sems
        self.dnext = 0
        self.gsems = [es.enter_context(nc.semaphore("gm%d" % i)) for i in range(10)]
        self.guse = [0] * 10
        self.gnext = 0
        self.out_sem = es.enter_context(nc.semaphore("outs"))
        self.out_uses = 0
        self.last_w = {}
        self.readers = {}
        self.waited = {e: {} for e in ENGINES}
        self.seen_keys = set()
        self.graveyard = []
        self.n_inst = 0

    def _marks_for(self, reads, writes):
        marks = []
        for r in reads:
            m = self.last_w.get(r)
            if m is not None:
                marks.append(m)
        for w in writes:
            m = self.last_w.get(w)
            if m is not None:
                marks.append(m)
            for m2 in self.readers.get(w, {}).values():
                marks.append(m2)
            if w not in self.seen_keys:
                marks.extend(self.graveyard)
        return marks

    def _dedupe(self, eng, marks):
        out = []
        w = self.waited[eng]
        for (sem, val) in marks:
            k = id(sem)
            if w.get(k, 0) >= val:
                continue
            w[k] = val
            out.append((sem, val))
        return out

    def _commit(self, eng, mark, reads, writes):
        for wkey in writes:
            self.last_w[wkey] = mark
            self.readers[wkey] = {}
            self.seen_keys.add(wkey)
        for r in reads:
            d = self.readers.setdefault(r, {})
            k = id(mark[0])
            if k not in d or d[k][1] < mark[1]:
                d[k] = mark

    def close_scope(self, keys):
        for k in keys:
            m = self.last_w.get(k)
            if m is not None:
                self.graveyard.append(m)
            for m2 in self.readers.get(k, {}).values():
                self.graveyard.append(m2)
        best = {}
        for (sem, val) in self.graveyard:
            if id(sem) not in best or best[id(sem)][1] < val:
                best[id(sem)] = (sem, val)
        self.graveyard = list(best.values())

    def op(self, eng, fn, reads=(), writes=()):
        return self.macro(eng, [fn], reads=reads, writes=writes)

    def macro(self, eng, fns, reads=(), writes=()):
        reads = list(reads)
        writes = list(writes)
        waits = self._dedupe(eng, self._marks_for(reads, writes))
        self.ecount[eng] += 1
        mark = (self.esem[eng], self.ecount[eng])
        sem = self.esem[eng]
        self.n_inst += len(fns) + len(waits)

        def emit(E, fns=fns, waits=waits, sem=sem):
            for (s, v) in waits:
                E.wait_ge(s, v)
            inst = None
            for f in fns:
                inst = f(E)
            inst.then_inc(sem, 1)
        self.ops[eng].append(emit)
        self._commit(eng, mark, reads, writes)
        return mark

    def raw(self, eng, fn):
        self.ops[eng].append(fn)
        self.n_inst += 1

    def dma(self, eng, fn, reads=(), writes=()):
        reads = list(reads)
        writes = list(writes)
        if eng == "gpsimd":
            j = self.gnext
            self.gnext = (self.gnext + 1) % len(self.gsems)
            s = self.gsems[j]
            use = self.guse
        else:
            j = self.dnext
            self.dnext = (self.dnext + 1) % len(self.dsems)
            s = self.dsems[j]
            use = self.duse
        marks = self._marks_for(reads, writes)
        if use[j] > 0:
            marks.append((s, 16 * use[j]))
        waits = self._dedupe(eng, marks)
        use[j] += 1
        mark = (s, 16 * use[j])
        self.n_inst += 1 + len(waits)

        def emit(E, fn=fn, waits=waits, s=s):
            for (ss, v) in waits:
                E.wait_ge(ss, v)
            fn(E).then_inc(s, 16)
        self.ops[eng].append(emit)
        self._commit(eng, mark, reads, writes)
        return mark

    def dma_out(self, eng, fn, reads=()):
        reads = list(reads)
        waits = self._dedupe(eng, self._marks_for(reads, []))
        self.out_uses += 1
        mark = (self.out_sem, 16 * self.out_uses)
        self.n_inst += 1 + len(waits)

        def emit(E, fn=fn, waits=waits, s=self.out_sem):
            for (ss, v) in waits:
                E.wait_ge(ss, v)
            fn(E).then_inc(s, 16)
        self.ops[eng].append(emit)
        for r in reads:
            d = self.readers.setdefault(r, {})
            k = id(mark[0])
            if k not in d or d[k][1] < mark[1]:
                d[k] = mark
        return mark


class TS:

    def __init__(self, name, dram, row0, col0, width):
        self.name = name
        self.dram = dram
        self.row0 = row0
        self.col0 = col0
        self.width = width


def make_stream(p, demands, pool, pool_tag, wcast, wcast_tag, pf=2):
    info = {}
    state = {"emitted": 0, "lc": 0}

    def emit_one(i):
        ts = demands[i]
        sl = i % len(pool)
        lw = ts.width
        wc = state["lc"] % len(wcast)
        state["lc"] += 1
        kcast = ("wcast", wcast_tag, wc)
        kT = ("wT", pool_tag, sl)
        src = ts.dram[ts.row0:ts.row0 + 128, ts.col0: ts.col0 + lw]
        p.dma("gpsimd",
              lambda E, d=wcast[wc], s=src, lw=lw:
              E.dma_start(out=d[:, 0:lw], in_=s),
              reads=[], writes=[kcast])
        out_ap = (pool[sl][:, 0:lw]
                  .rearrange("q (c e) -> q c e", c=lw // 128))
        p.dma("sync",
              lambda E, o=out_ap, ii=wcast[wc], lw=lw:
              E.dma_start_transpose(out=o, in_=ii[:, 0:lw]),
              reads=[kcast], writes=[kT])
        info[i] = (sl, [kT])

    def get(i):
        hi = min(i + pf, len(demands) - 1)
        while state["emitted"] <= hi:
            emit_one(state["emitted"])
            state["emitted"] += 1
        return info[i]

    return get


def build_nc():
    nc = bass.Bass("TRN2", target_bir_lowering=False)

    dx = nc.dram_tensor("x", [T, D], F32, kind="ExternalInput")
    dwq = nc.dram_tensor("wq", [D, D], F32, kind="ExternalInput")
    dwk = nc.dram_tensor("wk", [D, D], F32, kind="ExternalInput")
    dwv = nc.dram_tensor("wv", [D, D], F32, kind="ExternalInput")
    dwo = nc.dram_tensor("wo", [D, D], F32, kind="ExternalInput")
    dgw = nc.dram_tensor("gw", [INTER, D], F32, kind="ExternalInput")
    duw = nc.dram_tensor("uw", [INTER, D], F32, kind="ExternalInput")
    ddw = nc.dram_tensor("dw", [D, INTER], F32, kind="ExternalInput")
    dbq = nc.dram_tensor("bq", [128, 16], F32, kind="ExternalInput")
    dbk = nc.dram_tensor("bk", [128, 16], F32, kind="ExternalInput")
    dbv = nc.dram_tensor("bv", [128, 16], F32, kind="ExternalInput")
    dbo = nc.dram_tensor("bo", [128, 16], F32, kind="ExternalInput")
    dgs = nc.dram_tensor("gs", [128, 44], F32, kind="ExternalInput")
    dus = nc.dram_tensor("us", [128, 44], F32, kind="ExternalInput")
    dds = nc.dram_tensor("ds", [128, 16], F32, kind="ExternalInput")
    dsendm = nc.dram_tensor("sendm", [128, 16], F32, kind="ExternalInput")
    dusem = nc.dram_tensor("usem", [128, 16], F32, kind="ExternalInput")
    dident = nc.dram_tensor("ident", [128, 128], F32, kind="ExternalInput")
    dy = nc.dram_tensor("y", [T, D], F32, kind="ExternalOutput")
    dccin = nc.dram_tensor("ccin", [128, 16], F32)
    dccout = nc.dram_tensor("ccout", [128, 16], F32)

    es = contextlib.ExitStack()
    with es:
        p = Prog(nc, es)
        sb = nc.sbuf_tensor

        def alive(guard):
            return guard.__enter__(), guard

        s_bq = es.enter_context(sb("s_bq", [128, 16], F32))
        s_bk = es.enter_context(sb("s_bk", [128, 16], F32))
        s_bv = es.enter_context(sb("s_bv", [128, 16], F32))
        s_bo = es.enter_context(sb("s_bo", [128, 16], F32))
        s_gs = es.enter_context(sb("s_gs", [128, 44], F32))
        s_us = es.enter_context(sb("s_us", [128, 44], F32))
        s_ds = es.enter_context(sb("s_ds", [128, 16], F32))
        s_sendm = es.enter_context(sb("s_sendm", [128, 16], F32))
        s_usem = es.enter_context(sb("s_usem", [128, 16], F32))
        s_ls = es.enter_context(sb("s_ls", [128, 16], F32))
        s_ls2 = es.enter_context(sb("s_ls2", [128, 2], F32))
        s_send = es.enter_context(sb("s_send", [128, 16], F32))
        s_crecv = es.enter_context(sb("s_crecv", [128, 16], F32))
        s_carry = es.enter_context(sb("s_carry", [128, 16], F32))
        s_stat = es.enter_context(sb("s_stat", [128, 8], F32))
        s_eps = es.enter_context(sb("s_eps", [128, 1], F32))
        s_ones = es.enter_context(sb("s_ones", [128, T], F32))
        s_xn = es.enter_context(sb("s_xn", [128, ECN * T], BF16))
        s_ctx, g_ctx = alive(sb("ctx", [128, ECN * T], F16))

        pp = [es.enter_context(nc.psum_tensor("pp%d" % i, [128, TW], F32))
              for i in range(6)]
        pt = [es.enter_context(nc.psum_tensor("pt%d" % i, [128, TW], F32))
              for i in range(2)]

        ppi = [0]

        def next_pp():
            i = ppi[0] % 6
            ppi[0] += 1
            return i

        def mm_group(pslot, pairs, reads):
            fns = []
            n = len(pairs)
            for k, (st, mv) in enumerate(pairs):
                fns.append(lambda E, st=st, mv=mv, s=(k == 0), e=(k == n - 1):
                           E.matmul(pp[pslot][:], st, mv, start=s, stop=e))
            return p.macro("tensor", fns, reads=reads, writes=[("pp", pslot)])

        for key, dsrc, sdst in (
            (("bq",), dbq, s_bq), (("bk",), dbk, s_bk), (("bv",), dbv, s_bv),
            (("bo",), dbo, s_bo), (("gs",), dgs, s_gs), (("us",), dus, s_us),
            (("ds",), dds, s_ds), (("sendm",), dsendm, s_sendm),
            (("usem",), dusem, s_usem),
        ):
            p.dma("sync", lambda E, d=sdst, s=dsrc: E.dma_start(out=d[:], in_=s[:, :]),
                  reads=[], writes=[key])
        p.op("gpsimd", lambda E: E.memset(s_ones[:], 1.0), reads=[], writes=[("ones",)])
        p.op("gpsimd", lambda E: E.memset(s_eps[:], EPS), reads=[], writes=[("eps",)])

        s_wcast, g_wcast = zip(*[alive(sb("wcast%d" % i, [128, 2048], BF16)) for i in range(3)])
        s_wT, g_wT = zip(*[alive(sb("wTa%d" % i, [128, 2048], BF16)) for i in range(4)])
        s_t1, g_t1 = alive(sb("t1", [128, TW], F32))
        s_t2, g_t2 = alive(sb("t2", [128, TW], F32))
        s_t3, g_t3 = alive(sb("t3", [128, TW], F32))
        s_kvt, g_kvt = alive(sb("kvt", [128, T], F32))

        s_xl, g_xl = zip(*[alive(sb("xl%d" % i, [128, D], F32)) for i in range(2)])
        s_xnt, g_xnt = zip(*[alive(sb("xnt%d" % i, [128, D], BF16)) for i in range(2)])
        s_sq, g_sq = alive(sb("sq", [128, D], F32))

        for tt in range(8):
            sl = tt % 2
            kx = ("xl", sl)
            p.dma("sync", lambda E, d=s_xl[sl], tt=tt:
                  E.dma_start(out=d[:], in_=dx[tt * 128:(tt + 1) * 128, :]),
                  reads=[], writes=[kx])
            p.op("vector", lambda E, ii=s_xl[sl]:
                 E.reduce_sum(s_stat[:, 0:1], ii[:], axis=mybir.AxisListType.X),
                 reads=[kx], writes=[("st", 0)])
            p.op("scalar", lambda E, ii=s_xl[sl]:
                 E.activation(s_sq[:], ii[:], AF.Square, accum_out=s_stat[:, 1:2]),
                 reads=[kx], writes=[("sq",), ("st", 1)])
            p.op("vector", lambda E:
                 E.tensor_scalar_mul(s_stat[:, 2:3], s_stat[:, 0:1], -1.0 / D),
                 reads=[("st", 0)], writes=[("st", 2)])
            p.op("vector", lambda E:
                 E.tensor_mul(s_stat[:, 3:4], s_stat[:, 2:3], s_stat[:, 2:3]),
                 reads=[("st", 2)], writes=[("st", 3)])
            p.op("vector", lambda E:
                 E.tensor_scalar_mul(s_stat[:, 4:5], s_stat[:, 1:2], 1.0 / D),
                 reads=[("st", 1)], writes=[("st", 4)])
            p.op("vector", lambda E:
                 E.tensor_sub(s_stat[:, 5:6], s_stat[:, 4:5], s_stat[:, 3:4]),
                 reads=[("st", 3), ("st", 4)], writes=[("st", 5)])
            p.op("scalar", lambda E:
                 E.activation(s_stat[:, 6:7], s_stat[:, 5:6], AF.Sqrt, bias=s_eps[:]),
                 reads=[("st", 5), ("eps",)], writes=[("st", 6)])
            p.op("vector", lambda E:
                 E.reciprocal(s_stat[:, 7:8], s_stat[:, 6:7]),
                 reads=[("st", 6)], writes=[("st", 7)])
            kxn = ("xnt", sl)
            p.op("vector", lambda E, ii=s_xl[sl], o=s_xnt[sl]:
                 E.tensor_scalar(o[:], ii[:], s_stat[:, 2:3], s_stat[:, 7:8],
                                 ALU.add, ALU.mult),
                 reads=[kx, ("st", 2), ("st", 7)], writes=[kxn])
            out_ap = (s_xn[:].rearrange("q (c t) -> q c t", c=ECN)
                      [:, :, tt * 128:(tt + 1) * 128])
            p.dma("scalar", lambda E, o=out_ap, ii=s_xnt[sl]:
                  E.dma_start_transpose(out=o, in_=ii[:]),
                  reads=[kxn], writes=[("xn", tt)])

        p.close_scope([("xl", 0), ("xl", 1), ("xnt", 0), ("xnt", 1), ("sq",)])
        g_sq.__exit__(None, None, None)
        for g in reversed(list(g_xnt)):
            g.__exit__(None, None, None)
        for g in reversed(list(g_xl)):
            g.__exit__(None, None, None)

        def xn_reads(tc):
            return [("xn", tt) for tt in range(tc * 4, tc * 4 + 4)]

        def mov_xn(dc, tc):
            return s_xn[:, dc * T + tc * TW: dc * T + tc * TW + TW]

        s_pkv, g_pkv = alive(sb("pkv", [128, ECN * T], F32))
        s_phiqf, g_phiqf = alive(sb("phiqf", [128, 8 * T], F32))

        def phi_evac(pslot, bias, out_ap, out_key):
            kp = ("pp", pslot)
            p.op("vector", lambda E, b=bias, pslot=pslot:
                 E.tensor_scalar(s_t1[:], pp[pslot][:], b, 0.0, ALU.add, ALU.min),
                 reads=[kp], writes=[("t1",)])
            p.op("scalar", lambda E:
                 E.activation(s_t2[:], s_t1[:], AF.Exp),
                 reads=[("t1",)], writes=[("t2",)])
            p.op("vector", lambda E, b=bias, pslot=pslot:
                 E.tensor_scalar(s_t3[:], pp[pslot][:], b, 0.0, ALU.add, ALU.max),
                 reads=[kp], writes=[("t3",)])
            p.op("vector", lambda E, o=out_ap:
                 E.tensor_add(o, s_t3[:], s_t2[:]),
                 reads=[("t2",), ("t3",)], writes=[out_key])

        kv_demands = []
        for ec in range(ECN):
            kv_demands.append(TS(("K", ec), dwk, ec * 128, 0, D))
            kv_demands.append(TS(("V", ec), dwv, ec * 128, 0, D))
        kv_get = make_stream(p, kv_demands, s_wT, "a", s_wcast, "a", pf=2)

        for ec in range(ECN):
            slk, rk = kv_get(2 * ec)
            slv, rv = kv_get(2 * ec + 1)
            for tc in range(TCN):
                ps = next_pp()
                mm_group(ps, [(s_wT[slk][:, dc * 128:(dc + 1) * 128], mov_xn(dc, tc))
                              for dc in range(ECN)], reads=rk + xn_reads(tc))
                phi_evac(ps, s_bk[:, ec:ec + 1], s_t1[:], ("t1",))
                ps2 = next_pp()
                mm_group(ps2, [(s_wT[slv][:, dc * 128:(dc + 1) * 128], mov_xn(dc, tc))
                               for dc in range(ECN)], reads=rv + xn_reads(tc))
                p.op("vector", lambda E, b=s_bv[:, ec:ec + 1], ps2=ps2:
                     E.tensor_scalar(s_t3[:], pp[ps2][:], b, None, ALU.add),
                     reads=[("pp", ps2)], writes=[("t3",)])
                p.op("vector", lambda E, ec=ec, tc=tc:
                     E.scalar_tensor_tensor(
                         s_pkv[:, ec * T + tc * TW: ec * T + (tc + 1) * TW],
                         s_t1[:], 1.0, s_t3[:], ALU.mult, ALU.mult,
                         accum_out=s_ls2[:, tc:tc + 1]),
                     reads=[("t1",), ("t3",)],
                     writes=[("pkv", ec, tc), ("ls2", tc)])
            p.op("vector", lambda E, ec=ec:
                 E.tensor_add(s_ls[:, ec:ec + 1], s_ls2[:, 0:1], s_ls2[:, 1:2]),
                 reads=[("ls2", 0), ("ls2", 1)], writes=[("ls", ec)])

        p.op("vector", lambda E:
             E.tensor_mul(s_send[:], s_ls[:], s_sendm[:]),
             reads=[("sendm",)] + [("ls", ec) for ec in range(ECN)],
             writes=[("send",)])

        q_demands = [TS(("Q", ec), dwq, ec * 128, 0, D) for ec in range(ECN)]
        q_get = make_stream(p, q_demands, s_wT, "a", s_wcast, "a", pf=2)
        q_get(0)
        q_get(1)

        p.dma("gpsimd", lambda E: E.dma_start(out=dccin[:, :], in_=s_send[:]),
              reads=[("send",)], writes=[("ccin",)])
        p.op("gpsimd", lambda E:
             E.collective_compute(
                 "AllReduce", ALU.add,
                 replica_groups=[[0, 1], [2, 3], [4, 5], [6, 7]],
                 ins=[dccin.ap().opt()], outs=[dccout.ap().opt()]),
             reads=[("ccin",)], writes=[("ccout",)])
        p.dma("gpsimd", lambda E: E.dma_start(out=s_crecv[:], in_=dccout[:, :]),
              reads=[("ccout",)], writes=[("crecv",)])
        p.op("vector", lambda E:
             E.tensor_mul(s_carry[:], s_crecv[:], s_usem[:]),
             reads=[("crecv",), ("usem",)], writes=[("carry",)])

        for half in range(2):
            for e8 in range(8):
                ec = half * 8 + e8
                sl, rk = q_get(ec)
                for tc in range(TCN):
                    ps = next_pp()
                    mm_group(ps, [(s_wT[sl][:, dc * 128:(dc + 1) * 128], mov_xn(dc, tc))
                                  for dc in range(ECN)], reads=rk + xn_reads(tc))
                    phi_evac(ps, s_bq[:, ec:ec + 1],
                             s_phiqf[:, e8 * T + tc * TW: e8 * T + (tc + 1) * TW],
                             ("phiqf", e8, tc))
            for e8 in range(8):
                ec = half * 8 + e8
                p.op("vector", lambda E, ec=ec:
                     E.tensor_tensor_scan(
                         s_kvt[:], s_ones[:],
                         s_pkv[:, ec * T:(ec + 1) * T], s_carry[:, ec:ec + 1],
                         ALU.mult, ALU.add),
                     reads=[("ones",), ("carry",), ("pkv", ec, 0), ("pkv", ec, 1)],
                     writes=[("kvt",)])
                p.op("vector", lambda E, ec=ec, e8=e8:
                     E.scalar_tensor_tensor(
                         s_ctx[:, ec * T:(ec + 1) * T],
                         s_phiqf[:, e8 * T:(e8 + 1) * T], SCALE, s_kvt[:],
                         ALU.mult, ALU.mult),
                     reads=[("kvt",), ("phiqf", e8, 0), ("phiqf", e8, 1)],
                     writes=[("ctx", ec)])

        p.close_scope(
            [("pkv", ec, tc) for ec in range(ECN) for tc in range(TCN)]
            + [("kvt",), ("t1",), ("t2",), ("t3",)]
            + [("phiqf", e8, tc) for e8 in range(8) for tc in range(TCN)]
            + [("wT", "a", sl) for sl in range(4)]
            + [("wcast", "a", i) for i in range(3)])
        g_phiqf.__exit__(None, None, None)
        g_pkv.__exit__(None, None, None)
        g_kvt.__exit__(None, None, None)
        g_t3.__exit__(None, None, None)
        g_t2.__exit__(None, None, None)
        g_t1.__exit__(None, None, None)
        for g in reversed(list(g_wT)):
            g.__exit__(None, None, None)
        for g in reversed(list(g_wcast)):
            g.__exit__(None, None, None)

        s_acc, g_acc = alive(sb("s_acc", [128, ECN * T], F32))
        s_ocast, g_ocast = zip(*[alive(sb("ocast%d" % i, [128, 2048], F16)) for i in range(3)])
        s_owT, g_owT = zip(*[alive(sb("owT%d" % i, [128, 2048], F16)) for i in range(3)])

        o_demands = [TS(("O", ec), dwo, ec * 128, 0, D) for ec in range(ECN)]
        o_get = make_stream(p, o_demands, s_owT, "o", s_ocast, "o", pf=2)

        def mov_ctx(dc, tc):
            return s_ctx[:, dc * T + tc * TW: dc * T + tc * TW + TW]

        ctx_reads = [("ctx", ec) for ec in range(ECN)]
        for ec in range(ECN):
            sl, rk = o_get(ec)
            for tc in range(TCN):
                ps = next_pp()
                mm_group(ps, [(s_owT[sl][:, dc * 128:(dc + 1) * 128], mov_ctx(dc, tc))
                              for dc in range(ECN)], reads=rk + ctx_reads)
                p.op("vector", lambda E, ec=ec, tc=tc, ps=ps:
                     E.tensor_scalar(
                         s_acc[:, ec * T + tc * TW: ec * T + (tc + 1) * TW],
                         pp[ps][:], s_bo[:, ec:ec + 1], None, ALU.add),
                     reads=[("pp", ps), ("bo",)], writes=[("acc", ec, tc)])

        p.close_scope(
            [("wT", "o", sl) for sl in range(3)]
            + [("wcast", "o", i) for i in range(3)])
        for g in reversed(list(g_owT)):
            g.__exit__(None, None, None)
        for g in reversed(list(g_ocast)):
            g.__exit__(None, None, None)

        s_hdn, g_hdn = alive(sb("hdn", [128, ICQ * T], BF16))
        s_fc, g_fc = zip(*[alive(sb("fc%d" % i, [128, 2048], BF16)) for i in range(3)])
        s_gu, g_gu = zip(*[alive(sb("gu%d" % i, [128, 2048], BF16)) for i in range(4)])
        s_dwT, g_dwT = zip(*[alive(sb("dwT%d" % i, [128, 1408], BF16)) for i in range(2)])
        s_sg, g_sg = alive(sb("sgt", [128, TW], F32))
        s_su, g_su = alive(sb("sut", [128, TW], F32))

        for q in range(QN):
            gu_demands = []
            for il in range(ICQ):
                ic = q * ICQ + il
                gu_demands.append(TS(("G", ic), dgw, ic * 128, 0, D))
                gu_demands.append(TS(("U", ic), duw, ic * 128, 0, D))
            gu_get = make_stream(p, gu_demands, s_gu, ("gu", q), s_fc, "f", pf=2)

            for il in range(ICQ):
                ic = q * ICQ + il
                slg, rg = gu_get(2 * il)
                slu, ru = gu_get(2 * il + 1)
                for tc in range(TCN):
                    ps = next_pp()
                    mm_group(ps, [(s_gu[slg][:, dc * 128:(dc + 1) * 128], mov_xn(dc, tc))
                                  for dc in range(ECN)], reads=rg + xn_reads(tc))
                    p.op("scalar", lambda E, ic=ic, ps=ps:
                         E.activation(s_sg[:], pp[ps][:], AF.Silu,
                                      scale=s_gs[:, ic:ic + 1]),
                         reads=[("pp", ps), ("gs",)], writes=[("sg",)])
                    ps2 = next_pp()
                    mm_group(ps2, [(s_gu[slu][:, dc * 128:(dc + 1) * 128], mov_xn(dc, tc))
                                   for dc in range(ECN)], reads=ru + xn_reads(tc))
                    p.op("vector", lambda E, ic=ic, ps2=ps2:
                         E.tensor_scalar(s_su[:], pp[ps2][:], s_us[:, ic:ic + 1],
                                         None, ALU.mult),
                         reads=[("pp", ps2), ("us",)], writes=[("su",)])
                    p.op("vector", lambda E, il=il, tc=tc:
                         E.tensor_mul(
                             s_hdn[:, il * T + tc * TW: il * T + (tc + 1) * TW],
                             s_sg[:], s_su[:]),
                         reads=[("sg",), ("su",)], writes=[("hdn", il, tc)])

            dn_demands = [TS(("Dn", hc, q), ddw, hc * 128, q * 1408, 1408)
                          for hc in range(ECN)]
            dn_get = make_stream(p, dn_demands, s_dwT, ("dn", q), s_fc, "f", pf=1)
            hdn_reads = [("hdn", il, tc) for il in range(ICQ) for tc in range(TCN)]
            for hc in range(ECN):
                sl, rk = dn_get(hc)
                for tc in range(TCN):
                    ps = next_pp()
                    mm_group(ps, [(s_dwT[sl][:, il * 128:(il + 1) * 128],
                                   s_hdn[:, il * T + tc * TW: il * T + tc * TW + TW])
                                  for il in range(ICQ)], reads=rk + hdn_reads)
                    p.op("vector", lambda E, hc=hc, tc=tc, ps=ps:
                         E.scalar_tensor_tensor(
                             s_acc[:, hc * T + tc * TW: hc * T + (tc + 1) * TW],
                             pp[ps][:], s_ds[:, hc:hc + 1],
                             s_acc[:, hc * T + tc * TW: hc * T + (tc + 1) * TW],
                             ALU.mult, ALU.add),
                         reads=[("pp", ps), ("ds",), ("acc", hc, tc)],
                         writes=[("acc", hc, tc)])

        ffn_keys = ([("hdn", il, tc) for il in range(ICQ) for tc in range(TCN)]
                    + [("sg",), ("su",)]
                    + [("wcast", "f", i) for i in range(3)])
        for q in range(QN):
            ffn_keys += [("wT", ("gu", q), sl) for sl in range(4)]
            ffn_keys += [("wT", ("dn", q), sl) for sl in range(2)]
        p.close_scope(ffn_keys)
        g_su.__exit__(None, None, None)
        g_sg.__exit__(None, None, None)
        for g in reversed(list(g_dwT)):
            g.__exit__(None, None, None)
        for g in reversed(list(g_gu)):
            g.__exit__(None, None, None)
        for g in reversed(list(g_fc)):
            g.__exit__(None, None, None)
        g_hdn.__exit__(None, None, None)

        s_xt, g_xt = zip(*[alive(sb("xt%d" % i, [128, D], F32)) for i in range(2)])
        s_yt, g_yt = zip(*[alive(sb("yt%d" % i, [128, D], F32)) for i in range(2)])
        s_id, g_id = alive(sb("identity", [128, 128], F32))
        p.dma("sync", lambda E: E.dma_start(out=s_id[:], in_=dident[:, :]),
              reads=[], writes=[("id",)])
        out_marks = []

        for tt in range(8):
            sl = tt % 2
            kx = ("xt", sl)
            p.dma("sync", lambda E, d=s_xt[sl], tt=tt:
                  E.dma_start(out=d[:], in_=dx[tt * 128:(tt + 1) * 128, :]),
                  reads=[], writes=[kx])
            tc = tt // 4
            for dc in range(ECN):
                pslot = dc % 2
                p.op("tensor", lambda E, dc=dc, tt=tt, pslot=pslot:
                     E.transpose(pt[pslot][:, 0:128],
                                 s_acc[:, dc * T + tt * 128: dc * T + (tt + 1) * 128],
                                 s_id[:]),
                     reads=[("id",), ("acc", dc, tc)], writes=[("pt", pslot)])
                p.op("vector", lambda E, dc=dc, sl=sl, pslot=pslot:
                     E.tensor_add(s_yt[sl][:, dc * 128:(dc + 1) * 128],
                                  s_xt[sl][:, dc * 128:(dc + 1) * 128],
                                  pt[pslot][:, 0:128]),
                     reads=[kx, ("pt", pslot)], writes=[("yt", sl, dc)])
            out_marks.append(p.dma("sync", lambda E, sl=sl, tt=tt:
                      E.dma_start(out=dy[tt * 128:(tt + 1) * 128, :], in_=s_yt[sl][:]),
                      reads=[("yt", sl, dc) for dc in range(ECN)]))

        final = {}
        for (sem, val) in out_marks:
            if id(sem) not in final or final[id(sem)][1] < val:
                final[id(sem)] = (sem, val)
        for (sem, val) in final.values():
            p.raw("sync", lambda E, sem=sem, val=val: E.wait_ge(sem, val))

        g_id.__exit__(None, None, None)
        for g in reversed(list(g_yt)):
            g.__exit__(None, None, None)
        for g in reversed(list(g_xt)):
            g.__exit__(None, None, None)
        g_acc.__exit__(None, None, None)
        g_ctx.__exit__(None, None, None)

        with nc.Block() as block:
            block.sync(lambda E: [f(E) for f in p.ops["sync"]])
            block.scalar(lambda E: [f(E) for f in p.ops["scalar"]])
            block.vector(lambda E: [f(E) for f in p.ops["vector"]])
            block.tensor(lambda E: [f(E) for f in p.ops["tensor"]])
            block.gpsimd(lambda E: [f(E) for f in p.ops["gpsimd"]])

    return nc


_NC = None


def _get_nc():
    global _NC
    if _NC is None:
        _NC = build_nc()
    return _NC


def kernel(L0_attn_x, L0_attn_Wq, L0_attn_bq, L0_attn_Wk, L0_attn_bk,
           L0_attn_Wv, L0_attn_bv, L0_attn_Wo, L0_attn_bo,
           L0_ffn_gate_w, L0_ffn_up_w, L0_ffn_down_w,
           L0_ffn_gate_s, L0_ffn_up_s, L0_ffn_down_s):
    X = np.ascontiguousarray(np.asarray(L0_attn_x, dtype=np.float32))
    wq = np.ascontiguousarray(np.asarray(L0_attn_Wq, dtype=np.float32))
    wk = np.ascontiguousarray(np.asarray(L0_attn_Wk, dtype=np.float32))
    wv = np.ascontiguousarray(np.asarray(L0_attn_Wv, dtype=np.float32))
    wo = np.ascontiguousarray(np.asarray(L0_attn_Wo, dtype=np.float32))
    gw = np.ascontiguousarray(np.asarray(L0_ffn_gate_w, dtype=np.float32))
    uw = np.ascontiguousarray(np.asarray(L0_ffn_up_w, dtype=np.float32))
    dwn = np.ascontiguousarray(np.asarray(L0_ffn_down_w, dtype=np.float32))

    def pcol(v, n):
        return np.ascontiguousarray(np.asarray(v, dtype=np.float32).reshape(n, 128).T)

    bq = pcol(L0_attn_bq, 16)
    bk = pcol(L0_attn_bk, 16)
    bv = pcol(L0_attn_bv, 16)
    bo = pcol(L0_attn_bo, 16)
    gs = pcol(L0_ffn_gate_s, 44)
    us = pcol(L0_ffn_up_s, 44)
    ds = pcol(L0_ffn_down_s, 16)
    ident = np.eye(128, dtype=np.float32)

    nc = _get_nc()
    in_maps = []
    for c in range(NCORES):
        b, h = c // 2, c % 2
        in_maps.append({
            "x": np.ascontiguousarray(X[b, h * T:(h + 1) * T]),
            "wq": wq, "wk": wk, "wv": wv, "wo": wo,
            "gw": gw, "uw": uw, "dw": dwn,
            "bq": bq, "bk": bk, "bv": bv, "bo": bo,
            "gs": gs, "us": us, "ds": ds,
            "sendm": np.full((128, 16), 1.0 if h == 0 else 0.0, np.float32),
            "usem": np.full((128, 16), 1.0 if h == 1 else 0.0, np.float32),
            "ident": ident,
        })
    res = run_bass_kernel_spmd(nc, in_maps, core_ids=list(range(NCORES)))
    out = np.empty((4, 2048, D), np.float32)
    for c in range(NCORES):
        b, h = c // 2, c % 2
        out[b, h * T:(h + 1) * T] = res.results[c]["y"]
    return out


# version 08
# speedup vs baseline: 1.4472x; 1.1527x over previous
import contextlib
import numpy as np

import concourse.bass as bass
import concourse.mybir as mybir
from concourse.bass_utils import run_bass_kernel_spmd

F32 = mybir.dt.float32
BF16 = mybir.dt.bfloat16
F16 = mybir.dt.float16
AF = mybir.ActivationFunctionType
ALU = mybir.AluOpType

D = 2048
T = 1024
ECN = 16
TW = 512
TCN = T // TW
INTER = 5632
ICN = INTER // 128
QN = 4
ICQ = ICN // QN
SCALE = float(128.0 ** -0.5)
NCORES = 8
EPS = 1e-5

ENGINES = ("sync", "scalar", "vector", "tensor", "gpsimd")


class Prog:

    def __init__(self, nc, es, n_dma_sems=24):
        self.nc = nc
        self.ops = {e: [] for e in ENGINES}
        self.esem = {e: es.enter_context(nc.semaphore("es_" + e)) for e in ENGINES}
        self.ecount = {e: 0 for e in ENGINES}
        self.dsems = [es.enter_context(nc.semaphore("dm%d" % i)) for i in range(n_dma_sems)]
        self.duse = [0] * n_dma_sems
        self.dnext = 0
        self.gsems = [es.enter_context(nc.semaphore("gm%d" % i)) for i in range(10)]
        self.guse = [0] * 10
        self.gnext = 0
        self.out_sem = es.enter_context(nc.semaphore("outs"))
        self.out_uses = 0
        self.last_w = {}
        self.readers = {}
        self.waited = {e: {} for e in ENGINES}
        self.seen_keys = set()
        self.graveyard = []
        self.n_inst = 0

    def _marks_for(self, reads, writes):
        marks = []
        for r in reads:
            m = self.last_w.get(r)
            if m is not None:
                marks.append(m)
        for w in writes:
            m = self.last_w.get(w)
            if m is not None:
                marks.append(m)
            for m2 in self.readers.get(w, {}).values():
                marks.append(m2)
            if w not in self.seen_keys:
                marks.extend(self.graveyard)
        return marks

    def _dedupe(self, eng, marks):
        out = []
        w = self.waited[eng]
        for (sem, val) in marks:
            k = id(sem)
            if w.get(k, 0) >= val:
                continue
            w[k] = val
            out.append((sem, val))
        return out

    def _commit(self, eng, mark, reads, writes):
        for wkey in writes:
            self.last_w[wkey] = mark
            self.readers[wkey] = {}
            self.seen_keys.add(wkey)
        for r in reads:
            d = self.readers.setdefault(r, {})
            k = id(mark[0])
            if k not in d or d[k][1] < mark[1]:
                d[k] = mark

    def close_scope(self, keys):
        for k in keys:
            m = self.last_w.get(k)
            if m is not None:
                self.graveyard.append(m)
            for m2 in self.readers.get(k, {}).values():
                self.graveyard.append(m2)
        best = {}
        for (sem, val) in self.graveyard:
            if id(sem) not in best or best[id(sem)][1] < val:
                best[id(sem)] = (sem, val)
        self.graveyard = list(best.values())

    def op(self, eng, fn, reads=(), writes=()):
        return self.macro(eng, [fn], reads=reads, writes=writes)

    def macro(self, eng, fns, reads=(), writes=()):
        reads = list(reads)
        writes = list(writes)
        waits = self._dedupe(eng, self._marks_for(reads, writes))
        self.ecount[eng] += 1
        mark = (self.esem[eng], self.ecount[eng])
        sem = self.esem[eng]
        self.n_inst += len(fns) + len(waits)

        def emit(E, fns=fns, waits=waits, sem=sem):
            for (s, v) in waits:
                E.wait_ge(s, v)
            inst = None
            for f in fns:
                inst = f(E)
            inst.then_inc(sem, 1)
        self.ops[eng].append(emit)
        self._commit(eng, mark, reads, writes)
        return mark

    def raw(self, eng, fn):
        self.ops[eng].append(fn)
        self.n_inst += 1

    def dma(self, eng, fn, reads=(), writes=()):
        reads = list(reads)
        writes = list(writes)
        if eng == "gpsimd":
            j = self.gnext
            self.gnext = (self.gnext + 1) % len(self.gsems)
            s = self.gsems[j]
            use = self.guse
        else:
            j = self.dnext
            self.dnext = (self.dnext + 1) % len(self.dsems)
            s = self.dsems[j]
            use = self.duse
        marks = self._marks_for(reads, writes)
        if use[j] > 0:
            marks.append((s, 16 * use[j]))
        waits = self._dedupe(eng, marks)
        use[j] += 1
        mark = (s, 16 * use[j])
        self.n_inst += 1 + len(waits)

        def emit(E, fn=fn, waits=waits, s=s):
            for (ss, v) in waits:
                E.wait_ge(ss, v)
            fn(E).then_inc(s, 16)
        self.ops[eng].append(emit)
        self._commit(eng, mark, reads, writes)
        return mark

    def dma_out(self, eng, fn, reads=()):
        reads = list(reads)
        waits = self._dedupe(eng, self._marks_for(reads, []))
        self.out_uses += 1
        mark = (self.out_sem, 16 * self.out_uses)
        self.n_inst += 1 + len(waits)

        def emit(E, fn=fn, waits=waits, s=self.out_sem):
            for (ss, v) in waits:
                E.wait_ge(ss, v)
            fn(E).then_inc(s, 16)
        self.ops[eng].append(emit)
        for r in reads:
            d = self.readers.setdefault(r, {})
            k = id(mark[0])
            if k not in d or d[k][1] < mark[1]:
                d[k] = mark
        return mark


class TS:

    def __init__(self, name, dram, row0, col0, width):
        self.name = name
        self.dram = dram
        self.row0 = row0
        self.col0 = col0
        self.width = width


def make_stream(p, demands, pool, pool_tag, wcast, wcast_tag, pf=2):
    info = {}
    state = {"emitted": 0, "lc": 0}

    def emit_one(i):
        ts = demands[i]
        sl = i % len(pool)
        lw = ts.width
        wc = state["lc"] % len(wcast)
        state["lc"] += 1
        kcast = ("wcast", wcast_tag, wc)
        kT = ("wT", pool_tag, sl)
        src = ts.dram[ts.row0:ts.row0 + 128, ts.col0: ts.col0 + lw]
        p.dma("gpsimd",
              lambda E, d=wcast[wc], s=src, lw=lw:
              E.dma_start(out=d[:, 0:lw], in_=s),
              reads=[], writes=[kcast])
        out_ap = (pool[sl][:, 0:lw]
                  .rearrange("q (c e) -> q c e", c=lw // 128))
        p.dma("sync",
              lambda E, o=out_ap, ii=wcast[wc], lw=lw:
              E.dma_start_transpose(out=o, in_=ii[:, 0:lw]),
              reads=[kcast], writes=[kT])
        info[i] = (sl, [kT])

    def get(i):
        hi = min(i + pf, len(demands) - 1)
        while state["emitted"] <= hi:
            emit_one(state["emitted"])
            state["emitted"] += 1
        return info[i]

    return get


def build_nc():
    nc = bass.Bass("TRN2", target_bir_lowering=False)

    dx = nc.dram_tensor("x", [T, D], F32, kind="ExternalInput")
    dwq = nc.dram_tensor("wq", [D, D], F32, kind="ExternalInput")
    dwk = nc.dram_tensor("wk", [D, D], F32, kind="ExternalInput")
    dwv = nc.dram_tensor("wv", [D, D], F32, kind="ExternalInput")
    dwo = nc.dram_tensor("wo", [D, D], F32, kind="ExternalInput")
    dgw = nc.dram_tensor("gw", [INTER, D], F32, kind="ExternalInput")
    duw = nc.dram_tensor("uw", [INTER, D], F32, kind="ExternalInput")
    ddw = nc.dram_tensor("dw", [D, INTER], F32, kind="ExternalInput")
    dbq = nc.dram_tensor("bq", [128, 16], F32, kind="ExternalInput")
    dbk = nc.dram_tensor("bk", [128, 16], F32, kind="ExternalInput")
    dbv = nc.dram_tensor("bv", [128, 16], F32, kind="ExternalInput")
    dbo = nc.dram_tensor("bo", [128, 16], F32, kind="ExternalInput")
    dgs = nc.dram_tensor("gs", [128, 44], F32, kind="ExternalInput")
    dus = nc.dram_tensor("us", [128, 44], F32, kind="ExternalInput")
    dds = nc.dram_tensor("ds", [128, 16], F32, kind="ExternalInput")
    dsendm = nc.dram_tensor("sendm", [128, 16], F32, kind="ExternalInput")
    dusem = nc.dram_tensor("usem", [128, 16], F32, kind="ExternalInput")
    dident = nc.dram_tensor("ident", [128, 128], F32, kind="ExternalInput")
    dy = nc.dram_tensor("y", [T, D], F32, kind="ExternalOutput")
    dccin = nc.dram_tensor("ccin", [128, 16], F32)
    dccout = nc.dram_tensor("ccout", [128, 16], F32)

    es = contextlib.ExitStack()
    with es:
        p = Prog(nc, es)
        sb = nc.sbuf_tensor

        def alive(guard):
            return guard.__enter__(), guard

        s_bq = es.enter_context(sb("s_bq", [128, 16], F32))
        s_bk = es.enter_context(sb("s_bk", [128, 16], F32))
        s_bv = es.enter_context(sb("s_bv", [128, 16], F32))
        s_bo = es.enter_context(sb("s_bo", [128, 16], F32))
        s_gs = es.enter_context(sb("s_gs", [128, 44], F32))
        s_us = es.enter_context(sb("s_us", [128, 44], F32))
        s_ds = es.enter_context(sb("s_ds", [128, 16], F32))
        s_sendm = es.enter_context(sb("s_sendm", [128, 16], F32))
        s_usem = es.enter_context(sb("s_usem", [128, 16], F32))
        s_ls = es.enter_context(sb("s_ls", [128, 16], F32))
        s_ls2 = es.enter_context(sb("s_ls2", [128, 2], F32))
        s_send = es.enter_context(sb("s_send", [128, 16], F32))
        s_crecv = es.enter_context(sb("s_crecv", [128, 16], F32))
        s_carry = es.enter_context(sb("s_carry", [128, 16], F32))
        s_stat = es.enter_context(sb("s_stat", [128, 8], F32))
        s_eps = es.enter_context(sb("s_eps", [128, 1], F32))
        s_ones = es.enter_context(sb("s_ones", [128, T], F32))
        s_xn = es.enter_context(sb("s_xn", [128, ECN * T], BF16))
        s_ctx, g_ctx = alive(sb("ctx", [128, ECN * T], F16))

        pp = [es.enter_context(nc.psum_tensor("pp%d" % i, [128, TW], F32))
              for i in range(6)]
        pt = [es.enter_context(nc.psum_tensor("pt%d" % i, [128, TW], F32))
              for i in range(2)]

        ppi = [0]

        def next_pp():
            i = ppi[0] % 6
            ppi[0] += 1
            return i

        def mm_group(pslot, pairs, reads):
            fns = []
            n = len(pairs)
            for k, (st, mv) in enumerate(pairs):
                fns.append(lambda E, st=st, mv=mv, s=(k == 0), e=(k == n - 1):
                           E.matmul(pp[pslot][:], st, mv, start=s, stop=e))
            return p.macro("tensor", fns, reads=reads, writes=[("pp", pslot)])

        for key, dsrc, sdst in (
            (("bq",), dbq, s_bq), (("bk",), dbk, s_bk), (("bv",), dbv, s_bv),
            (("bo",), dbo, s_bo), (("gs",), dgs, s_gs), (("us",), dus, s_us),
            (("ds",), dds, s_ds), (("sendm",), dsendm, s_sendm),
            (("usem",), dusem, s_usem),
        ):
            p.dma("sync", lambda E, d=sdst, s=dsrc: E.dma_start(out=d[:], in_=s[:, :]),
                  reads=[], writes=[key])
        p.op("gpsimd", lambda E: E.memset(s_ones[:], 1.0), reads=[], writes=[("ones",)])
        p.op("gpsimd", lambda E: E.memset(s_eps[:], EPS), reads=[], writes=[("eps",)])

        s_wcast, g_wcast = zip(*[alive(sb("wcast%d" % i, [128, 2048], BF16)) for i in range(3)])
        s_wT, g_wT = zip(*[alive(sb("wTa%d" % i, [128, 2048], BF16)) for i in range(5)])
        s_t1, g_t1 = alive(sb("t1", [128, TW], F32))
        s_t2, g_t2 = alive(sb("t2", [128, TW], F32))
        s_t3, g_t3 = alive(sb("t3", [128, TW], F32))
        s_kvt, g_kvt = alive(sb("kvt", [128, T], F32))

        s_xl, g_xl = zip(*[alive(sb("xl%d" % i, [128, D], F32)) for i in range(2)])
        s_xnt, g_xnt = zip(*[alive(sb("xnt%d" % i, [128, D], BF16)) for i in range(2)])
        s_sq, g_sq = alive(sb("sq", [128, D], F32))

        for tt in range(8):
            sl = tt % 2
            kx = ("xl", sl)
            p.dma("sync", lambda E, d=s_xl[sl], tt=tt:
                  E.dma_start(out=d[:], in_=dx[tt * 128:(tt + 1) * 128, :]),
                  reads=[], writes=[kx])
            p.op("vector", lambda E, ii=s_xl[sl]:
                 E.reduce_sum(s_stat[:, 0:1], ii[:], axis=mybir.AxisListType.X),
                 reads=[kx], writes=[("st", 0)])
            p.op("scalar", lambda E, ii=s_xl[sl]:
                 E.activation(s_sq[:], ii[:], AF.Square, accum_out=s_stat[:, 1:2]),
                 reads=[kx], writes=[("sq",), ("st", 1)])
            p.op("vector", lambda E:
                 E.tensor_scalar_mul(s_stat[:, 2:3], s_stat[:, 0:1], -1.0 / D),
                 reads=[("st", 0)], writes=[("st", 2)])
            p.op("vector", lambda E:
                 E.tensor_mul(s_stat[:, 3:4], s_stat[:, 2:3], s_stat[:, 2:3]),
                 reads=[("st", 2)], writes=[("st", 3)])
            p.op("vector", lambda E:
                 E.tensor_scalar_mul(s_stat[:, 4:5], s_stat[:, 1:2], 1.0 / D),
                 reads=[("st", 1)], writes=[("st", 4)])
            p.op("vector", lambda E:
                 E.tensor_sub(s_stat[:, 5:6], s_stat[:, 4:5], s_stat[:, 3:4]),
                 reads=[("st", 3), ("st", 4)], writes=[("st", 5)])
            p.op("scalar", lambda E:
                 E.activation(s_stat[:, 6:7], s_stat[:, 5:6], AF.Sqrt, bias=s_eps[:]),
                 reads=[("st", 5), ("eps",)], writes=[("st", 6)])
            p.op("vector", lambda E:
                 E.reciprocal(s_stat[:, 7:8], s_stat[:, 6:7]),
                 reads=[("st", 6)], writes=[("st", 7)])
            kxn = ("xnt", sl)
            p.op("vector", lambda E, ii=s_xl[sl], o=s_xnt[sl]:
                 E.tensor_scalar(o[:], ii[:], s_stat[:, 2:3], s_stat[:, 7:8],
                                 ALU.add, ALU.mult),
                 reads=[kx, ("st", 2), ("st", 7)], writes=[kxn])
            out_ap = (s_xn[:].rearrange("q (c t) -> q c t", c=ECN)
                      [:, :, tt * 128:(tt + 1) * 128])
            p.dma("scalar", lambda E, o=out_ap, ii=s_xnt[sl]:
                  E.dma_start_transpose(out=o, in_=ii[:]),
                  reads=[kxn], writes=[("xn", tt)])

        p.close_scope([("xl", 0), ("xl", 1), ("xnt", 0), ("xnt", 1), ("sq",)])
        g_sq.__exit__(None, None, None)
        for g in reversed(list(g_xnt)):
            g.__exit__(None, None, None)
        for g in reversed(list(g_xl)):
            g.__exit__(None, None, None)

        def xn_reads(tc):
            return [("xn", tt) for tt in range(tc * 4, tc * 4 + 4)]

        def mov_xn(dc, tc):
            return s_xn[:, dc * T + tc * TW: dc * T + tc * TW + TW]

        s_pkv, g_pkv = alive(sb("pkv", [128, ECN * T], F32))
        s_phiqf, g_phiqf = alive(sb("phiqf", [128, 8 * T], F32))

        def phi_evac(pslot, bias, out_ap, out_key):
            kp = ("pp", pslot)
            p.op("vector", lambda E, b=bias, pslot=pslot:
                 E.tensor_scalar(s_t1[:], pp[pslot][:], b, 0.0, ALU.add, ALU.min),
                 reads=[kp], writes=[("t1",)])
            p.op("scalar", lambda E:
                 E.activation(s_t2[:], s_t1[:], AF.Exp),
                 reads=[("t1",)], writes=[("t2",)])
            p.op("vector", lambda E, b=bias, pslot=pslot:
                 E.tensor_scalar(s_t3[:], pp[pslot][:], b, 0.0, ALU.add, ALU.max),
                 reads=[kp], writes=[("t3",)])
            p.op("vector", lambda E, o=out_ap:
                 E.tensor_add(o, s_t3[:], s_t2[:]),
                 reads=[("t2",), ("t3",)], writes=[out_key])

        kv_demands = []
        for ec in range(ECN):
            kv_demands.append(TS(("K", ec), dwk, ec * 128, 0, D))
            kv_demands.append(TS(("V", ec), dwv, ec * 128, 0, D))
        kv_get = make_stream(p, kv_demands, s_wT, "a", s_wcast, "a", pf=3)

        for ec in range(ECN):
            slk, rk = kv_get(2 * ec)
            slv, rv = kv_get(2 * ec + 1)
            for tc in range(TCN):
                ps = next_pp()
                mm_group(ps, [(s_wT[slk][:, dc * 128:(dc + 1) * 128], mov_xn(dc, tc))
                              for dc in range(ECN)], reads=rk + xn_reads(tc))
                phi_evac(ps, s_bk[:, ec:ec + 1], s_t1[:], ("t1",))
                ps2 = next_pp()
                mm_group(ps2, [(s_wT[slv][:, dc * 128:(dc + 1) * 128], mov_xn(dc, tc))
                               for dc in range(ECN)], reads=rv + xn_reads(tc))
                p.op("vector", lambda E, b=s_bv[:, ec:ec + 1], ps2=ps2:
                     E.tensor_scalar(s_t3[:], pp[ps2][:], b, None, ALU.add),
                     reads=[("pp", ps2)], writes=[("t3",)])
                p.op("vector", lambda E, ec=ec, tc=tc:
                     E.scalar_tensor_tensor(
                         s_pkv[:, ec * T + tc * TW: ec * T + (tc + 1) * TW],
                         s_t1[:], 1.0, s_t3[:], ALU.mult, ALU.mult,
                         accum_out=s_ls2[:, tc:tc + 1]),
                     reads=[("t1",), ("t3",)],
                     writes=[("pkv", ec, tc), ("ls2", tc)])
            p.op("vector", lambda E, ec=ec:
                 E.tensor_add(s_ls[:, ec:ec + 1], s_ls2[:, 0:1], s_ls2[:, 1:2]),
                 reads=[("ls2", 0), ("ls2", 1)], writes=[("ls", ec)])

        p.op("vector", lambda E:
             E.tensor_mul(s_send[:], s_ls[:], s_sendm[:]),
             reads=[("sendm",)] + [("ls", ec) for ec in range(ECN)],
             writes=[("send",)])

        q_demands = [TS(("Q", ec), dwq, ec * 128, 0, D) for ec in range(ECN)]
        q_get = make_stream(p, q_demands, s_wT, "a", s_wcast, "a", pf=3)
        q_get(0)
        q_get(1)

        p.dma("gpsimd", lambda E: E.dma_start(out=dccin[:, :], in_=s_send[:]),
              reads=[("send",)], writes=[("ccin",)])
        p.op("gpsimd", lambda E:
             E.collective_compute(
                 "AllReduce", ALU.add,
                 replica_groups=[[0, 1], [2, 3], [4, 5], [6, 7]],
                 ins=[dccin.ap().opt()], outs=[dccout.ap().opt()]),
             reads=[("ccin",)], writes=[("ccout",)])
        p.dma("gpsimd", lambda E: E.dma_start(out=s_crecv[:], in_=dccout[:, :]),
              reads=[("ccout",)], writes=[("crecv",)])
        p.op("vector", lambda E:
             E.tensor_mul(s_carry[:], s_crecv[:], s_usem[:]),
             reads=[("crecv",), ("usem",)], writes=[("carry",)])

        for half in range(2):
            for e8 in range(8):
                ec = half * 8 + e8
                sl, rk = q_get(ec)
                for tc in range(TCN):
                    ps = next_pp()
                    mm_group(ps, [(s_wT[sl][:, dc * 128:(dc + 1) * 128], mov_xn(dc, tc))
                                  for dc in range(ECN)], reads=rk + xn_reads(tc))
                    phi_evac(ps, s_bq[:, ec:ec + 1],
                             s_phiqf[:, e8 * T + tc * TW: e8 * T + (tc + 1) * TW],
                             ("phiqf", e8, tc))
            for e8 in range(8):
                ec = half * 8 + e8
                p.op("vector", lambda E, ec=ec:
                     E.tensor_tensor_scan(
                         s_kvt[:], s_ones[:],
                         s_pkv[:, ec * T:(ec + 1) * T], s_carry[:, ec:ec + 1],
                         ALU.mult, ALU.add),
                     reads=[("ones",), ("carry",), ("pkv", ec, 0), ("pkv", ec, 1)],
                     writes=[("kvt",)])
                p.op("vector", lambda E, ec=ec, e8=e8:
                     E.scalar_tensor_tensor(
                         s_ctx[:, ec * T:(ec + 1) * T],
                         s_phiqf[:, e8 * T:(e8 + 1) * T], SCALE, s_kvt[:],
                         ALU.mult, ALU.mult),
                     reads=[("kvt",), ("phiqf", e8, 0), ("phiqf", e8, 1)],
                     writes=[("ctx", ec)])

        p.close_scope(
            [("pkv", ec, tc) for ec in range(ECN) for tc in range(TCN)]
            + [("kvt",), ("t1",), ("t2",), ("t3",)]
            + [("phiqf", e8, tc) for e8 in range(8) for tc in range(TCN)]
            + [("wT", "a", sl) for sl in range(5)]
            + [("wcast", "a", i) for i in range(3)])
        g_phiqf.__exit__(None, None, None)
        g_pkv.__exit__(None, None, None)
        g_kvt.__exit__(None, None, None)
        g_t3.__exit__(None, None, None)
        g_t2.__exit__(None, None, None)
        g_t1.__exit__(None, None, None)
        for g in reversed(list(g_wT)):
            g.__exit__(None, None, None)
        for g in reversed(list(g_wcast)):
            g.__exit__(None, None, None)

        s_acc, g_acc = alive(sb("s_acc", [128, ECN * T], F32))
        s_ocast, g_ocast = zip(*[alive(sb("ocast%d" % i, [128, 2048], F16)) for i in range(3)])
        s_owT, g_owT = zip(*[alive(sb("owT%d" % i, [128, 2048], F16)) for i in range(3)])

        o_demands = [TS(("O", ec), dwo, ec * 128, 0, D) for ec in range(ECN)]
        o_get = make_stream(p, o_demands, s_owT, "o", s_ocast, "o", pf=2)

        def mov_ctx(dc, tc):
            return s_ctx[:, dc * T + tc * TW: dc * T + tc * TW + TW]

        ctx_reads = [("ctx", ec) for ec in range(ECN)]
        for ec in range(ECN):
            sl, rk = o_get(ec)
            for tc in range(TCN):
                ps = next_pp()
                mm_group(ps, [(s_owT[sl][:, dc * 128:(dc + 1) * 128], mov_ctx(dc, tc))
                              for dc in range(ECN)], reads=rk + ctx_reads)
                p.op("vector", lambda E, ec=ec, tc=tc, ps=ps:
                     E.tensor_scalar(
                         s_acc[:, ec * T + tc * TW: ec * T + (tc + 1) * TW],
                         pp[ps][:], s_bo[:, ec:ec + 1], None, ALU.add),
                     reads=[("pp", ps), ("bo",)], writes=[("acc", ec, tc)])

        p.close_scope(
            [("wT", "o", sl) for sl in range(3)]
            + [("wcast", "o", i) for i in range(3)])
        for g in reversed(list(g_owT)):
            g.__exit__(None, None, None)
        for g in reversed(list(g_ocast)):
            g.__exit__(None, None, None)

        s_hdn, g_hdn = alive(sb("hdn", [128, ICQ * T], BF16))
        s_fc, g_fc = zip(*[alive(sb("fc%d" % i, [128, 2048], BF16)) for i in range(3)])
        s_gu, g_gu = zip(*[alive(sb("gu%d" % i, [128, 2048], BF16)) for i in range(4)])
        s_dwT, g_dwT = zip(*[alive(sb("dwT%d" % i, [128, 1408], BF16)) for i in range(2)])
        s_sg, g_sg = alive(sb("sgt", [128, TW], F32))
        s_su, g_su = alive(sb("sut", [128, TW], F32))

        for q in range(QN):
            gu_demands = []
            for il in range(ICQ):
                ic = q * ICQ + il
                gu_demands.append(TS(("G", ic), dgw, ic * 128, 0, D))
                gu_demands.append(TS(("U", ic), duw, ic * 128, 0, D))
            gu_get = make_stream(p, gu_demands, s_gu, ("gu", q), s_fc, "f", pf=2)

            for il in range(ICQ):
                ic = q * ICQ + il
                slg, rg = gu_get(2 * il)
                slu, ru = gu_get(2 * il + 1)
                for tc in range(TCN):
                    ps = next_pp()
                    mm_group(ps, [(s_gu[slg][:, dc * 128:(dc + 1) * 128], mov_xn(dc, tc))
                                  for dc in range(ECN)], reads=rg + xn_reads(tc))
                    p.op("scalar", lambda E, ic=ic, ps=ps:
                         E.activation(s_sg[:], pp[ps][:], AF.Silu,
                                      scale=s_gs[:, ic:ic + 1]),
                         reads=[("pp", ps), ("gs",)], writes=[("sg",)])
                    ps2 = next_pp()
                    mm_group(ps2, [(s_gu[slu][:, dc * 128:(dc + 1) * 128], mov_xn(dc, tc))
                                   for dc in range(ECN)], reads=ru + xn_reads(tc))
                    p.op("vector", lambda E, ic=ic, ps2=ps2:
                         E.tensor_scalar(s_su[:], pp[ps2][:], s_us[:, ic:ic + 1],
                                         None, ALU.mult),
                         reads=[("pp", ps2), ("us",)], writes=[("su",)])
                    p.op("vector", lambda E, il=il, tc=tc:
                         E.tensor_mul(
                             s_hdn[:, il * T + tc * TW: il * T + (tc + 1) * TW],
                             s_sg[:], s_su[:]),
                         reads=[("sg",), ("su",)], writes=[("hdn", il, tc)])

            dn_demands = [TS(("Dn", hc, q), ddw, hc * 128, q * 1408, 1408)
                          for hc in range(ECN)]
            dn_get = make_stream(p, dn_demands, s_dwT, ("dn", q), s_fc, "f", pf=1)
            hdn_reads = [("hdn", il, tc) for il in range(ICQ) for tc in range(TCN)]
            for hc in range(ECN):
                sl, rk = dn_get(hc)
                for tc in range(TCN):
                    ps = next_pp()
                    mm_group(ps, [(s_dwT[sl][:, il * 128:(il + 1) * 128],
                                   s_hdn[:, il * T + tc * TW: il * T + tc * TW + TW])
                                  for il in range(ICQ)], reads=rk + hdn_reads)
                    p.op("vector", lambda E, hc=hc, tc=tc, ps=ps:
                         E.scalar_tensor_tensor(
                             s_acc[:, hc * T + tc * TW: hc * T + (tc + 1) * TW],
                             pp[ps][:], s_ds[:, hc:hc + 1],
                             s_acc[:, hc * T + tc * TW: hc * T + (tc + 1) * TW],
                             ALU.mult, ALU.add),
                         reads=[("pp", ps), ("ds",), ("acc", hc, tc)],
                         writes=[("acc", hc, tc)])

        ffn_keys = ([("hdn", il, tc) for il in range(ICQ) for tc in range(TCN)]
                    + [("sg",), ("su",)]
                    + [("wcast", "f", i) for i in range(3)])
        for q in range(QN):
            ffn_keys += [("wT", ("gu", q), sl) for sl in range(4)]
            ffn_keys += [("wT", ("dn", q), sl) for sl in range(2)]
        p.close_scope(ffn_keys)
        g_su.__exit__(None, None, None)
        g_sg.__exit__(None, None, None)
        for g in reversed(list(g_dwT)):
            g.__exit__(None, None, None)
        for g in reversed(list(g_gu)):
            g.__exit__(None, None, None)
        for g in reversed(list(g_fc)):
            g.__exit__(None, None, None)
        g_hdn.__exit__(None, None, None)

        s_xt, g_xt = zip(*[alive(sb("xt%d" % i, [128, D], F32)) for i in range(3)])
        s_yt, g_yt = zip(*[alive(sb("yt%d" % i, [128, D], F32)) for i in range(3)])
        s_id, g_id = alive(sb("identity", [128, 128], F32))
        p.dma("sync", lambda E: E.dma_start(out=s_id[:], in_=dident[:, :]),
              reads=[], writes=[("id",)])
        out_marks = []

        for tt in range(8):
            sl = tt % 3
            kx = ("xt", sl)
            p.dma("sync", lambda E, d=s_xt[sl], tt=tt:
                  E.dma_start(out=d[:], in_=dx[tt * 128:(tt + 1) * 128, :]),
                  reads=[], writes=[kx])
            tc = tt // 4
            ptpool = [pt[0], pt[1], pp[4], pp[5]]
            for dc in range(ECN):
                pslot = dc % 4
                extra = [("pp", pslot + 2)] if pslot >= 2 else []
                p.op("tensor", lambda E, dc=dc, tt=tt, pslot=pslot, ptpool=ptpool:
                     E.transpose(ptpool[pslot][:, 0:128],
                                 s_acc[:, dc * T + tt * 128: dc * T + (tt + 1) * 128],
                                 s_id[:]),
                     reads=[("id",), ("acc", dc, tc)], writes=[("pt", pslot)] + extra)
                p.op("vector", lambda E, dc=dc, sl=sl, pslot=pslot, ptpool=ptpool:
                     E.tensor_add(s_yt[sl][:, dc * 128:(dc + 1) * 128],
                                  s_xt[sl][:, dc * 128:(dc + 1) * 128],
                                  ptpool[pslot][:, 0:128]),
                     reads=[kx, ("pt", pslot)], writes=[("yt", sl, dc)])
            out_marks.append(p.dma("sync", lambda E, sl=sl, tt=tt:
                      E.dma_start(out=dy[tt * 128:(tt + 1) * 128, :], in_=s_yt[sl][:]),
                      reads=[("yt", sl, dc) for dc in range(ECN)]))

        final = {}
        for (sem, val) in out_marks:
            if id(sem) not in final or final[id(sem)][1] < val:
                final[id(sem)] = (sem, val)
        for (sem, val) in final.values():
            p.raw("sync", lambda E, sem=sem, val=val: E.wait_ge(sem, val))

        g_id.__exit__(None, None, None)
        for g in reversed(list(g_yt)):
            g.__exit__(None, None, None)
        for g in reversed(list(g_xt)):
            g.__exit__(None, None, None)
        g_acc.__exit__(None, None, None)
        g_ctx.__exit__(None, None, None)

        with nc.Block() as block:
            block.sync(lambda E: [f(E) for f in p.ops["sync"]])
            block.scalar(lambda E: [f(E) for f in p.ops["scalar"]])
            block.vector(lambda E: [f(E) for f in p.ops["vector"]])
            block.tensor(lambda E: [f(E) for f in p.ops["tensor"]])
            block.gpsimd(lambda E: [f(E) for f in p.ops["gpsimd"]])

    return nc


_NC = None


def _get_nc():
    global _NC
    if _NC is None:
        _NC = build_nc()
    return _NC


def kernel(L0_attn_x, L0_attn_Wq, L0_attn_bq, L0_attn_Wk, L0_attn_bk,
           L0_attn_Wv, L0_attn_bv, L0_attn_Wo, L0_attn_bo,
           L0_ffn_gate_w, L0_ffn_up_w, L0_ffn_down_w,
           L0_ffn_gate_s, L0_ffn_up_s, L0_ffn_down_s):
    X = np.ascontiguousarray(np.asarray(L0_attn_x, dtype=np.float32))
    wq = np.ascontiguousarray(np.asarray(L0_attn_Wq, dtype=np.float32))
    wk = np.ascontiguousarray(np.asarray(L0_attn_Wk, dtype=np.float32))
    wv = np.ascontiguousarray(np.asarray(L0_attn_Wv, dtype=np.float32))
    wo = np.ascontiguousarray(np.asarray(L0_attn_Wo, dtype=np.float32))
    gw = np.ascontiguousarray(np.asarray(L0_ffn_gate_w, dtype=np.float32))
    uw = np.ascontiguousarray(np.asarray(L0_ffn_up_w, dtype=np.float32))
    dwn = np.ascontiguousarray(np.asarray(L0_ffn_down_w, dtype=np.float32))

    def pcol(v, n):
        return np.ascontiguousarray(np.asarray(v, dtype=np.float32).reshape(n, 128).T)

    bq = pcol(L0_attn_bq, 16)
    bk = pcol(L0_attn_bk, 16)
    bv = pcol(L0_attn_bv, 16)
    bo = pcol(L0_attn_bo, 16)
    gs = pcol(L0_ffn_gate_s, 44)
    us = pcol(L0_ffn_up_s, 44)
    ds = pcol(L0_ffn_down_s, 16)
    ident = np.eye(128, dtype=np.float32)

    nc = _get_nc()
    in_maps = []
    for c in range(NCORES):
        b, h = c // 2, c % 2
        in_maps.append({
            "x": np.ascontiguousarray(X[b, h * T:(h + 1) * T]),
            "wq": wq, "wk": wk, "wv": wv, "wo": wo,
            "gw": gw, "uw": uw, "dw": dwn,
            "bq": bq, "bk": bk, "bv": bv, "bo": bo,
            "gs": gs, "us": us, "ds": ds,
            "sendm": np.full((128, 16), 1.0 if h == 0 else 0.0, np.float32),
            "usem": np.full((128, 16), 1.0 if h == 1 else 0.0, np.float32),
            "ident": ident,
        })
    res = run_bass_kernel_spmd(nc, in_maps, core_ids=list(range(NCORES)))
    out = np.empty((4, 2048, D), np.float32)
    for c in range(NCORES):
        b, h = c // 2, c % 2
        out[b, h * T:(h + 1) * T] = res.results[c]["y"]
    return out


# version 09 (final)
# speedup vs baseline: 1.4943x; 1.0326x over previous
import contextlib
import numpy as np

import concourse.bass as bass
import concourse.mybir as mybir
from concourse.bass_utils import run_bass_kernel_spmd

F32 = mybir.dt.float32
BF16 = mybir.dt.bfloat16
F16 = mybir.dt.float16
AF = mybir.ActivationFunctionType
ALU = mybir.AluOpType

D = 2048
T = 1024
ECN = 16
TW = 512
TCN = T // TW
INTER = 5632
ICN = INTER // 128
QN = 4
ICQ = ICN // QN
SCALE = float(128.0 ** -0.5)
NCORES = 8
EPS = 1e-5

ENGINES = ("sync", "scalar", "vector", "tensor", "gpsimd")


class Prog:

    def __init__(self, nc, es, n_dma_sems=24):
        self.nc = nc
        self.ops = {e: [] for e in ENGINES}
        self.esem = {e: es.enter_context(nc.semaphore("es_" + e)) for e in ENGINES}
        self.ecount = {e: 0 for e in ENGINES}
        self.dsems = [es.enter_context(nc.semaphore("dm%d" % i)) for i in range(n_dma_sems)]
        self.duse = [0] * n_dma_sems
        self.dnext = 0
        self.gsems = [es.enter_context(nc.semaphore("gm%d" % i)) for i in range(10)]
        self.guse = [0] * 10
        self.gnext = 0
        self.out_sem = es.enter_context(nc.semaphore("outs"))
        self.out_uses = 0
        self.last_w = {}
        self.readers = {}
        self.waited = {e: {} for e in ENGINES}
        self.seen_keys = set()
        self.graveyard = []
        self.n_inst = 0

    def _marks_for(self, reads, writes):
        marks = []
        for r in reads:
            m = self.last_w.get(r)
            if m is not None:
                marks.append(m)
        for w in writes:
            m = self.last_w.get(w)
            if m is not None:
                marks.append(m)
            for m2 in self.readers.get(w, {}).values():
                marks.append(m2)
            if w not in self.seen_keys:
                marks.extend(self.graveyard)
        return marks

    def _dedupe(self, eng, marks):
        out = []
        w = self.waited[eng]
        for (sem, val) in marks:
            k = id(sem)
            if w.get(k, 0) >= val:
                continue
            w[k] = val
            out.append((sem, val))
        return out

    def _commit(self, eng, mark, reads, writes):
        for wkey in writes:
            self.last_w[wkey] = mark
            self.readers[wkey] = {}
            self.seen_keys.add(wkey)
        for r in reads:
            d = self.readers.setdefault(r, {})
            k = id(mark[0])
            if k not in d or d[k][1] < mark[1]:
                d[k] = mark

    def close_scope(self, keys):
        for k in keys:
            m = self.last_w.get(k)
            if m is not None:
                self.graveyard.append(m)
            for m2 in self.readers.get(k, {}).values():
                self.graveyard.append(m2)
        best = {}
        for (sem, val) in self.graveyard:
            if id(sem) not in best or best[id(sem)][1] < val:
                best[id(sem)] = (sem, val)
        self.graveyard = list(best.values())

    def op(self, eng, fn, reads=(), writes=()):
        return self.macro(eng, [fn], reads=reads, writes=writes)

    def macro(self, eng, fns, reads=(), writes=()):
        reads = list(reads)
        writes = list(writes)
        waits = self._dedupe(eng, self._marks_for(reads, writes))
        self.ecount[eng] += 1
        mark = (self.esem[eng], self.ecount[eng])
        sem = self.esem[eng]
        self.n_inst += len(fns) + len(waits)

        def emit(E, fns=fns, waits=waits, sem=sem):
            for (s, v) in waits:
                E.wait_ge(s, v)
            inst = None
            for f in fns:
                inst = f(E)
            inst.then_inc(sem, 1)
        self.ops[eng].append(emit)
        self._commit(eng, mark, reads, writes)
        return mark

    def raw(self, eng, fn):
        self.ops[eng].append(fn)
        self.n_inst += 1

    def dma(self, eng, fn, reads=(), writes=()):
        reads = list(reads)
        writes = list(writes)
        if eng == "gpsimd":
            j = self.gnext
            self.gnext = (self.gnext + 1) % len(self.gsems)
            s = self.gsems[j]
            use = self.guse
        else:
            j = self.dnext
            self.dnext = (self.dnext + 1) % len(self.dsems)
            s = self.dsems[j]
            use = self.duse
        marks = self._marks_for(reads, writes)
        if use[j] > 0:
            marks.append((s, 16 * use[j]))
        waits = self._dedupe(eng, marks)
        use[j] += 1
        mark = (s, 16 * use[j])
        self.n_inst += 1 + len(waits)

        def emit(E, fn=fn, waits=waits, s=s):
            for (ss, v) in waits:
                E.wait_ge(ss, v)
            fn(E).then_inc(s, 16)
        self.ops[eng].append(emit)
        self._commit(eng, mark, reads, writes)
        return mark

    def dma_out(self, eng, fn, reads=()):
        reads = list(reads)
        waits = self._dedupe(eng, self._marks_for(reads, []))
        self.out_uses += 1
        mark = (self.out_sem, 16 * self.out_uses)
        self.n_inst += 1 + len(waits)

        def emit(E, fn=fn, waits=waits, s=self.out_sem):
            for (ss, v) in waits:
                E.wait_ge(ss, v)
            fn(E).then_inc(s, 16)
        self.ops[eng].append(emit)
        for r in reads:
            d = self.readers.setdefault(r, {})
            k = id(mark[0])
            if k not in d or d[k][1] < mark[1]:
                d[k] = mark
        return mark


class TS:

    def __init__(self, name, dram, row0, col0, width):
        self.name = name
        self.dram = dram
        self.row0 = row0
        self.col0 = col0
        self.width = width


def make_stream(p, demands, pool, pool_tag, wcast, wcast_tag, pf=2):
    info = {}
    state = {"emitted": 0, "lc": 0}

    def emit_one(i):
        ts = demands[i]
        sl = i % len(pool)
        lw = ts.width
        wc = state["lc"] % len(wcast)
        state["lc"] += 1
        kcast = ("wcast", wcast_tag, wc)
        kT = ("wT", pool_tag, sl)
        src = ts.dram[ts.row0:ts.row0 + 128, ts.col0: ts.col0 + lw]
        p.dma("gpsimd",
              lambda E, d=wcast[wc], s=src, lw=lw:
              E.dma_start(out=d[:, 0:lw], in_=s),
              reads=[], writes=[kcast])
        out_ap = (pool[sl][:, 0:lw]
                  .rearrange("q (c e) -> q c e", c=lw // 128))
        p.dma("sync",
              lambda E, o=out_ap, ii=wcast[wc], lw=lw:
              E.dma_start_transpose(out=o, in_=ii[:, 0:lw]),
              reads=[kcast], writes=[kT])
        info[i] = (sl, [kT])

    def get(i):
        hi = min(i + pf, len(demands) - 1)
        while state["emitted"] <= hi:
            emit_one(state["emitted"])
            state["emitted"] += 1
        return info[i]

    return get


def build_nc():
    nc = bass.Bass("TRN2", target_bir_lowering=False)

    dx = nc.dram_tensor("x", [T, D], F32, kind="ExternalInput")
    dwq = nc.dram_tensor("wq", [D, D], F32, kind="ExternalInput")
    dwk = nc.dram_tensor("wk", [D, D], F32, kind="ExternalInput")
    dwv = nc.dram_tensor("wv", [D, D], F32, kind="ExternalInput")
    dwo = nc.dram_tensor("wo", [D, D], F32, kind="ExternalInput")
    dgw = nc.dram_tensor("gw", [INTER, D], F32, kind="ExternalInput")
    duw = nc.dram_tensor("uw", [INTER, D], F32, kind="ExternalInput")
    ddw = nc.dram_tensor("dw", [D, INTER], F32, kind="ExternalInput")
    dbq = nc.dram_tensor("bq", [128, 16], F32, kind="ExternalInput")
    dbk = nc.dram_tensor("bk", [128, 16], F32, kind="ExternalInput")
    dbv = nc.dram_tensor("bv", [128, 16], F32, kind="ExternalInput")
    dbo = nc.dram_tensor("bo", [128, 16], F32, kind="ExternalInput")
    dgs = nc.dram_tensor("gs", [128, 44], F32, kind="ExternalInput")
    dus = nc.dram_tensor("us", [128, 44], F32, kind="ExternalInput")
    dds = nc.dram_tensor("ds", [128, 16], F32, kind="ExternalInput")
    dsendm = nc.dram_tensor("sendm", [128, 16], F32, kind="ExternalInput")
    dusem = nc.dram_tensor("usem", [128, 16], F32, kind="ExternalInput")
    dident = nc.dram_tensor("ident", [128, 128], F32, kind="ExternalInput")
    dy = nc.dram_tensor("y", [T, D], F32, kind="ExternalOutput")
    dccin = nc.dram_tensor("ccin", [128, 16], F32)
    dccout = nc.dram_tensor("ccout", [128, 16], F32)

    es = contextlib.ExitStack()
    with es:
        p = Prog(nc, es)
        sb = nc.sbuf_tensor

        def alive(guard):
            return guard.__enter__(), guard

        s_bq = es.enter_context(sb("s_bq", [128, 16], F32))
        s_bk = es.enter_context(sb("s_bk", [128, 16], F32))
        s_bv = es.enter_context(sb("s_bv", [128, 16], F32))
        s_bo = es.enter_context(sb("s_bo", [128, 16], F32))
        s_gs = es.enter_context(sb("s_gs", [128, 44], F32))
        s_us = es.enter_context(sb("s_us", [128, 44], F32))
        s_ds = es.enter_context(sb("s_ds", [128, 16], F32))
        s_sendm = es.enter_context(sb("s_sendm", [128, 16], F32))
        s_usem = es.enter_context(sb("s_usem", [128, 16], F32))
        s_ls = es.enter_context(sb("s_ls", [128, 16], F32))
        s_ls2 = es.enter_context(sb("s_ls2", [128, 2], F32))
        s_send = es.enter_context(sb("s_send", [128, 16], F32))
        s_crecv = es.enter_context(sb("s_crecv", [128, 16], F32))
        s_carry = es.enter_context(sb("s_carry", [128, 16], F32))
        s_stat = es.enter_context(sb("s_stat", [128, 8], F32))
        s_eps = es.enter_context(sb("s_eps", [128, 1], F32))
        s_ones = es.enter_context(sb("s_ones", [128, T], F32))
        s_xn = es.enter_context(sb("s_xn", [128, ECN * T], BF16))
        s_ctx, g_ctx = alive(sb("ctx", [128, ECN * T], F16))

        pp = [es.enter_context(nc.psum_tensor("pp%d" % i, [128, TW], F32))
              for i in range(6)]
        pt = [es.enter_context(nc.psum_tensor("pt%d" % i, [128, TW], F32))
              for i in range(2)]

        ppi = [0]

        def next_pp():
            i = ppi[0] % 6
            ppi[0] += 1
            return i

        def mm_group(pslot, pairs, reads):
            fns = []
            n = len(pairs)
            for k, (st, mv) in enumerate(pairs):
                fns.append(lambda E, st=st, mv=mv, s=(k == 0), e=(k == n - 1):
                           E.matmul(pp[pslot][:], st, mv, start=s, stop=e))
            return p.macro("tensor", fns, reads=reads, writes=[("pp", pslot)])

        for key, dsrc, sdst in (
            (("bq",), dbq, s_bq), (("bk",), dbk, s_bk), (("bv",), dbv, s_bv),
            (("bo",), dbo, s_bo), (("gs",), dgs, s_gs), (("us",), dus, s_us),
            (("ds",), dds, s_ds), (("sendm",), dsendm, s_sendm),
            (("usem",), dusem, s_usem),
        ):
            p.dma("sync", lambda E, d=sdst, s=dsrc: E.dma_start(out=d[:], in_=s[:, :]),
                  reads=[], writes=[key])
        p.op("gpsimd", lambda E: E.memset(s_ones[:], 1.0), reads=[], writes=[("ones",)])
        p.op("gpsimd", lambda E: E.memset(s_eps[:], EPS), reads=[], writes=[("eps",)])

        s_wcast, g_wcast = zip(*[alive(sb("wcast%d" % i, [128, 2048], BF16)) for i in range(3)])
        s_wT, g_wT = zip(*[alive(sb("wTa%d" % i, [128, 2048], BF16)) for i in range(5)])
        s_t1, g_t1 = alive(sb("t1", [128, TW], F32))
        s_t2, g_t2 = alive(sb("t2", [128, TW], F32))
        s_t3, g_t3 = alive(sb("t3", [128, TW], F32))
        s_kvt, g_kvt = alive(sb("kvt", [128, T], F32))

        s_xl, g_xl = zip(*[alive(sb("xl%d" % i, [128, D], F32)) for i in range(2)])
        s_xnt, g_xnt = zip(*[alive(sb("xnt%d" % i, [128, D], BF16)) for i in range(2)])
        s_sq, g_sq = alive(sb("sq", [128, D], F32))

        for tt in range(8):
            sl = tt % 2
            kx = ("xl", sl)
            p.dma("sync", lambda E, d=s_xl[sl], tt=tt:
                  E.dma_start(out=d[:], in_=dx[tt * 128:(tt + 1) * 128, :]),
                  reads=[], writes=[kx])
            p.op("vector", lambda E, ii=s_xl[sl]:
                 E.reduce_sum(s_stat[:, 0:1], ii[:], axis=mybir.AxisListType.X),
                 reads=[kx], writes=[("st", 0)])
            p.op("scalar", lambda E, ii=s_xl[sl]:
                 E.activation(s_sq[:], ii[:], AF.Square, accum_out=s_stat[:, 1:2]),
                 reads=[kx], writes=[("sq",), ("st", 1)])
            p.op("vector", lambda E:
                 E.tensor_scalar_mul(s_stat[:, 2:3], s_stat[:, 0:1], -1.0 / D),
                 reads=[("st", 0)], writes=[("st", 2)])
            p.op("vector", lambda E:
                 E.tensor_mul(s_stat[:, 3:4], s_stat[:, 2:3], s_stat[:, 2:3]),
                 reads=[("st", 2)], writes=[("st", 3)])
            p.op("vector", lambda E:
                 E.tensor_scalar_mul(s_stat[:, 4:5], s_stat[:, 1:2], 1.0 / D),
                 reads=[("st", 1)], writes=[("st", 4)])
            p.op("vector", lambda E:
                 E.tensor_sub(s_stat[:, 5:6], s_stat[:, 4:5], s_stat[:, 3:4]),
                 reads=[("st", 3), ("st", 4)], writes=[("st", 5)])
            p.op("scalar", lambda E:
                 E.activation(s_stat[:, 6:7], s_stat[:, 5:6], AF.Sqrt, bias=s_eps[:]),
                 reads=[("st", 5), ("eps",)], writes=[("st", 6)])
            p.op("vector", lambda E:
                 E.reciprocal(s_stat[:, 7:8], s_stat[:, 6:7]),
                 reads=[("st", 6)], writes=[("st", 7)])
            kxn = ("xnt", sl)
            p.op("vector", lambda E, ii=s_xl[sl], o=s_xnt[sl]:
                 E.tensor_scalar(o[:], ii[:], s_stat[:, 2:3], s_stat[:, 7:8],
                                 ALU.add, ALU.mult),
                 reads=[kx, ("st", 2), ("st", 7)], writes=[kxn])
            out_ap = (s_xn[:].rearrange("q (c t) -> q c t", c=ECN)
                      [:, :, tt * 128:(tt + 1) * 128])
            p.dma("scalar", lambda E, o=out_ap, ii=s_xnt[sl]:
                  E.dma_start_transpose(out=o, in_=ii[:]),
                  reads=[kxn], writes=[("xn", tt)])

        p.close_scope([("xl", 0), ("xl", 1), ("xnt", 0), ("xnt", 1), ("sq",)])
        g_sq.__exit__(None, None, None)
        for g in reversed(list(g_xnt)):
            g.__exit__(None, None, None)
        for g in reversed(list(g_xl)):
            g.__exit__(None, None, None)

        def xn_reads(tc):
            return [("xn", tt) for tt in range(tc * 4, tc * 4 + 4)]

        def mov_xn(dc, tc):
            return s_xn[:, dc * T + tc * TW: dc * T + tc * TW + TW]

        s_pkv, g_pkv = alive(sb("pkv", [128, ECN * T], F32))
        s_phiqf, g_phiqf = alive(sb("phiqf", [128, 8 * T], F32))

        def phi_evac(pslot, bias, out_ap, out_key):
            kp = ("pp", pslot)
            p.op("vector", lambda E, b=bias, pslot=pslot:
                 E.tensor_scalar(s_t1[:], pp[pslot][:], b, 0.0, ALU.add, ALU.min),
                 reads=[kp], writes=[("t1",)])
            p.op("scalar", lambda E:
                 E.activation(s_t2[:], s_t1[:], AF.Exp),
                 reads=[("t1",)], writes=[("t2",)])
            p.op("vector", lambda E, b=bias, pslot=pslot:
                 E.tensor_scalar(s_t3[:], pp[pslot][:], b, 0.0, ALU.add, ALU.max),
                 reads=[kp], writes=[("t3",)])
            p.op("vector", lambda E, o=out_ap:
                 E.tensor_add(o, s_t3[:], s_t2[:]),
                 reads=[("t2",), ("t3",)], writes=[out_key])

        kv_demands = []
        for ec in range(ECN):
            kv_demands.append(TS(("K", ec), dwk, ec * 128, 0, D))
            kv_demands.append(TS(("V", ec), dwv, ec * 128, 0, D))
        kv_get = make_stream(p, kv_demands, s_wT, "a", s_wcast, "a", pf=3)

        for ec in range(ECN):
            slk, rk = kv_get(2 * ec)
            slv, rv = kv_get(2 * ec + 1)
            for tc in range(TCN):
                ps = next_pp()
                mm_group(ps, [(s_wT[slk][:, dc * 128:(dc + 1) * 128], mov_xn(dc, tc))
                              for dc in range(ECN)], reads=rk + xn_reads(tc))
                phi_evac(ps, s_bk[:, ec:ec + 1], s_t1[:], ("t1",))
                ps2 = next_pp()
                mm_group(ps2, [(s_wT[slv][:, dc * 128:(dc + 1) * 128], mov_xn(dc, tc))
                               for dc in range(ECN)], reads=rv + xn_reads(tc))
                p.op("vector", lambda E, b=s_bv[:, ec:ec + 1], ps2=ps2:
                     E.tensor_scalar(s_t3[:], pp[ps2][:], b, None, ALU.add),
                     reads=[("pp", ps2)], writes=[("t3",)])
                p.op("vector", lambda E, ec=ec, tc=tc:
                     E.scalar_tensor_tensor(
                         s_pkv[:, ec * T + tc * TW: ec * T + (tc + 1) * TW],
                         s_t1[:], 1.0, s_t3[:], ALU.mult, ALU.mult,
                         accum_out=s_ls2[:, tc:tc + 1]),
                     reads=[("t1",), ("t3",)],
                     writes=[("pkv", ec, tc), ("ls2", tc)])
            p.op("vector", lambda E, ec=ec:
                 E.tensor_add(s_ls[:, ec:ec + 1], s_ls2[:, 0:1], s_ls2[:, 1:2]),
                 reads=[("ls2", 0), ("ls2", 1)], writes=[("ls", ec)])

        p.op("vector", lambda E:
             E.tensor_mul(s_send[:], s_ls[:], s_sendm[:]),
             reads=[("sendm",)] + [("ls", ec) for ec in range(ECN)],
             writes=[("send",)])

        q_demands = [TS(("Q", ec), dwq, ec * 128, 0, D) for ec in range(ECN)]
        q_get = make_stream(p, q_demands, s_wT, "a", s_wcast, "a", pf=3)
        q_get(0)
        q_get(1)

        p.dma("gpsimd", lambda E: E.dma_start(out=dccin[:, :], in_=s_send[:]),
              reads=[("send",)], writes=[("ccin",)])
        p.op("gpsimd", lambda E:
             E.collective_compute(
                 "AllReduce", ALU.add,
                 replica_groups=[[0, 1], [2, 3], [4, 5], [6, 7]],
                 ins=[dccin.ap().opt()], outs=[dccout.ap().opt()]),
             reads=[("ccin",)], writes=[("ccout",)])
        p.dma("gpsimd", lambda E: E.dma_start(out=s_crecv[:], in_=dccout[:, :]),
              reads=[("ccout",)], writes=[("crecv",)])
        for half in range(2):
            for e8 in range(8):
                ec = half * 8 + e8
                sl, rk = q_get(ec)
                for tc in range(TCN):
                    ps = next_pp()
                    mm_group(ps, [(s_wT[sl][:, dc * 128:(dc + 1) * 128], mov_xn(dc, tc))
                                  for dc in range(ECN)], reads=rk + xn_reads(tc))
                    phi_evac(ps, s_bq[:, ec:ec + 1],
                             s_phiqf[:, e8 * T + tc * TW: e8 * T + (tc + 1) * TW],
                             ("phiqf", e8, tc))
            if half == 0:
                p.op("vector", lambda E:
                     E.tensor_mul(s_carry[:], s_crecv[:], s_usem[:]),
                     reads=[("crecv",), ("usem",)], writes=[("carry",)])
            for e8 in range(8):
                ec = half * 8 + e8
                p.op("vector", lambda E, ec=ec:
                     E.tensor_tensor_scan(
                         s_kvt[:], s_ones[:],
                         s_pkv[:, ec * T:(ec + 1) * T], s_carry[:, ec:ec + 1],
                         ALU.mult, ALU.add),
                     reads=[("ones",), ("carry",), ("pkv", ec, 0), ("pkv", ec, 1)],
                     writes=[("kvt",)])
                p.op("vector", lambda E, ec=ec, e8=e8:
                     E.scalar_tensor_tensor(
                         s_ctx[:, ec * T:(ec + 1) * T],
                         s_phiqf[:, e8 * T:(e8 + 1) * T], SCALE, s_kvt[:],
                         ALU.mult, ALU.mult),
                     reads=[("kvt",), ("phiqf", e8, 0), ("phiqf", e8, 1)],
                     writes=[("ctx", ec)])

        p.close_scope(
            [("pkv", ec, tc) for ec in range(ECN) for tc in range(TCN)]
            + [("kvt",), ("t1",), ("t2",), ("t3",)]
            + [("phiqf", e8, tc) for e8 in range(8) for tc in range(TCN)]
            + [("wT", "a", sl) for sl in range(5)]
            + [("wcast", "a", i) for i in range(3)])
        g_phiqf.__exit__(None, None, None)
        g_pkv.__exit__(None, None, None)
        g_kvt.__exit__(None, None, None)
        g_t3.__exit__(None, None, None)
        g_t2.__exit__(None, None, None)
        g_t1.__exit__(None, None, None)
        for g in reversed(list(g_wT)):
            g.__exit__(None, None, None)
        for g in reversed(list(g_wcast)):
            g.__exit__(None, None, None)

        s_acc, g_acc = alive(sb("s_acc", [128, ECN * T], F32))
        s_ocast, g_ocast = zip(*[alive(sb("ocast%d" % i, [128, 2048], F16)) for i in range(3)])
        s_owT, g_owT = zip(*[alive(sb("owT%d" % i, [128, 2048], F16)) for i in range(3)])

        o_demands = [TS(("O", ec), dwo, ec * 128, 0, D) for ec in range(ECN)]
        o_get = make_stream(p, o_demands, s_owT, "o", s_ocast, "o", pf=2)

        def mov_ctx(dc, tc):
            return s_ctx[:, dc * T + tc * TW: dc * T + tc * TW + TW]

        ctx_reads = [("ctx", ec) for ec in range(ECN)]
        for ec in range(ECN):
            sl, rk = o_get(ec)
            for tc in range(TCN):
                ps = next_pp()
                mm_group(ps, [(s_owT[sl][:, dc * 128:(dc + 1) * 128], mov_ctx(dc, tc))
                              for dc in range(ECN)], reads=rk + ctx_reads)
                p.op("vector", lambda E, ec=ec, tc=tc, ps=ps:
                     E.tensor_scalar(
                         s_acc[:, ec * T + tc * TW: ec * T + (tc + 1) * TW],
                         pp[ps][:], s_bo[:, ec:ec + 1], None, ALU.add),
                     reads=[("pp", ps), ("bo",)], writes=[("acc", ec, tc)])

        p.close_scope(
            [("wT", "o", sl) for sl in range(3)]
            + [("wcast", "o", i) for i in range(3)])
        for g in reversed(list(g_owT)):
            g.__exit__(None, None, None)
        for g in reversed(list(g_ocast)):
            g.__exit__(None, None, None)

        s_hdn, g_hdn = alive(sb("hdn", [128, ICQ * T], BF16))
        s_fc, g_fc = zip(*[alive(sb("fc%d" % i, [128, 2048], BF16)) for i in range(3)])
        s_gu, g_gu = zip(*[alive(sb("gu%d" % i, [128, 2048], BF16)) for i in range(5)])
        s_dwT, g_dwT = zip(*[alive(sb("dwT%d" % i, [128, 1408], BF16)) for i in range(2)])
        s_sg, g_sg = alive(sb("sgt", [128, TW], F32))
        s_su, g_su = alive(sb("sut", [128, TW], F32))

        for q in range(QN):
            gu_demands = []
            for il in range(ICQ):
                ic = q * ICQ + il
                gu_demands.append(TS(("G", ic), dgw, ic * 128, 0, D))
                gu_demands.append(TS(("U", ic), duw, ic * 128, 0, D))
            gu_get = make_stream(p, gu_demands, s_gu, ("gu", q), s_fc, "f", pf=3)

            for il in range(ICQ):
                ic = q * ICQ + il
                slg, rg = gu_get(2 * il)
                slu, ru = gu_get(2 * il + 1)
                for tc in range(TCN):
                    ps = next_pp()
                    mm_group(ps, [(s_gu[slg][:, dc * 128:(dc + 1) * 128], mov_xn(dc, tc))
                                  for dc in range(ECN)], reads=rg + xn_reads(tc))
                    p.op("scalar", lambda E, ic=ic, ps=ps:
                         E.activation(s_sg[:], pp[ps][:], AF.Silu,
                                      scale=s_gs[:, ic:ic + 1]),
                         reads=[("pp", ps), ("gs",)], writes=[("sg",)])
                    ps2 = next_pp()
                    mm_group(ps2, [(s_gu[slu][:, dc * 128:(dc + 1) * 128], mov_xn(dc, tc))
                                   for dc in range(ECN)], reads=ru + xn_reads(tc))
                    p.op("vector", lambda E, ic=ic, ps2=ps2:
                         E.tensor_scalar(s_su[:], pp[ps2][:], s_us[:, ic:ic + 1],
                                         None, ALU.mult),
                         reads=[("pp", ps2), ("us",)], writes=[("su",)])
                    p.op("vector", lambda E, il=il, tc=tc:
                         E.tensor_mul(
                             s_hdn[:, il * T + tc * TW: il * T + (tc + 1) * TW],
                             s_sg[:], s_su[:]),
                         reads=[("sg",), ("su",)], writes=[("hdn", il, tc)])

            dn_demands = [TS(("Dn", hc, q), ddw, hc * 128, q * 1408, 1408)
                          for hc in range(ECN)]
            dn_get = make_stream(p, dn_demands, s_dwT, ("dn", q), s_fc, "f", pf=1)
            hdn_reads = [("hdn", il, tc) for il in range(ICQ) for tc in range(TCN)]
            for hc in range(ECN):
                sl, rk = dn_get(hc)
                for tc in range(TCN):
                    ps = next_pp()
                    mm_group(ps, [(s_dwT[sl][:, il * 128:(il + 1) * 128],
                                   s_hdn[:, il * T + tc * TW: il * T + tc * TW + TW])
                                  for il in range(ICQ)], reads=rk + hdn_reads)
                    p.op("vector", lambda E, hc=hc, tc=tc, ps=ps:
                         E.scalar_tensor_tensor(
                             s_acc[:, hc * T + tc * TW: hc * T + (tc + 1) * TW],
                             pp[ps][:], s_ds[:, hc:hc + 1],
                             s_acc[:, hc * T + tc * TW: hc * T + (tc + 1) * TW],
                             ALU.mult, ALU.add),
                         reads=[("pp", ps), ("ds",), ("acc", hc, tc)],
                         writes=[("acc", hc, tc)])

        ffn_keys = ([("hdn", il, tc) for il in range(ICQ) for tc in range(TCN)]
                    + [("sg",), ("su",)]
                    + [("wcast", "f", i) for i in range(3)])
        for q in range(QN):
            ffn_keys += [("wT", ("gu", q), sl) for sl in range(5)]
            ffn_keys += [("wT", ("dn", q), sl) for sl in range(2)]
        p.close_scope(ffn_keys)
        g_su.__exit__(None, None, None)
        g_sg.__exit__(None, None, None)
        for g in reversed(list(g_dwT)):
            g.__exit__(None, None, None)
        for g in reversed(list(g_gu)):
            g.__exit__(None, None, None)
        for g in reversed(list(g_fc)):
            g.__exit__(None, None, None)
        g_hdn.__exit__(None, None, None)

        s_xt, g_xt = zip(*[alive(sb("xt%d" % i, [128, D], F32)) for i in range(3)])
        s_yt, g_yt = zip(*[alive(sb("yt%d" % i, [128, D], F32)) for i in range(3)])
        s_id, g_id = alive(sb("identity", [128, 128], F32))
        p.dma("sync", lambda E: E.dma_start(out=s_id[:], in_=dident[:, :]),
              reads=[], writes=[("id",)])
        out_marks = []

        for tt in range(8):
            sl = tt % 3
            kx = ("xt", sl)
            p.dma("sync", lambda E, d=s_xt[sl], tt=tt:
                  E.dma_start(out=d[:], in_=dx[tt * 128:(tt + 1) * 128, :]),
                  reads=[], writes=[kx])
            tc = tt // 4
            ptpool = [pt[0], pt[1], pp[4], pp[5]]
            for dc in range(ECN):
                pslot = dc % 4
                extra = [("pp", pslot + 2)] if pslot >= 2 else []
                p.op("tensor", lambda E, dc=dc, tt=tt, pslot=pslot, ptpool=ptpool:
                     E.transpose(ptpool[pslot][:, 0:128],
                                 s_acc[:, dc * T + tt * 128: dc * T + (tt + 1) * 128],
                                 s_id[:]),
                     reads=[("id",), ("acc", dc, tc)], writes=[("pt", pslot)] + extra)
                p.op("vector", lambda E, dc=dc, sl=sl, pslot=pslot, ptpool=ptpool:
                     E.tensor_add(s_yt[sl][:, dc * 128:(dc + 1) * 128],
                                  s_xt[sl][:, dc * 128:(dc + 1) * 128],
                                  ptpool[pslot][:, 0:128]),
                     reads=[kx, ("pt", pslot)], writes=[("yt", sl, dc)])
            out_marks.append(p.dma("scalar", lambda E, sl=sl, tt=tt:
                      E.dma_start(out=dy[tt * 128:(tt + 1) * 128, :], in_=s_yt[sl][:]),
                      reads=[("yt", sl, dc) for dc in range(ECN)]))

        final = {}
        for (sem, val) in out_marks:
            if id(sem) not in final or final[id(sem)][1] < val:
                final[id(sem)] = (sem, val)
        for (sem, val) in final.values():
            p.raw("sync", lambda E, sem=sem, val=val: E.wait_ge(sem, val))

        g_id.__exit__(None, None, None)
        for g in reversed(list(g_yt)):
            g.__exit__(None, None, None)
        for g in reversed(list(g_xt)):
            g.__exit__(None, None, None)
        g_acc.__exit__(None, None, None)
        g_ctx.__exit__(None, None, None)

        with nc.Block() as block:
            block.sync(lambda E: [f(E) for f in p.ops["sync"]])
            block.scalar(lambda E: [f(E) for f in p.ops["scalar"]])
            block.vector(lambda E: [f(E) for f in p.ops["vector"]])
            block.tensor(lambda E: [f(E) for f in p.ops["tensor"]])
            block.gpsimd(lambda E: [f(E) for f in p.ops["gpsimd"]])

    return nc


_NC = None


def _get_nc():
    global _NC
    if _NC is None:
        _NC = build_nc()
    return _NC


def kernel(L0_attn_x, L0_attn_Wq, L0_attn_bq, L0_attn_Wk, L0_attn_bk,
           L0_attn_Wv, L0_attn_bv, L0_attn_Wo, L0_attn_bo,
           L0_ffn_gate_w, L0_ffn_up_w, L0_ffn_down_w,
           L0_ffn_gate_s, L0_ffn_up_s, L0_ffn_down_s):
    X = np.ascontiguousarray(np.asarray(L0_attn_x, dtype=np.float32))
    wq = np.ascontiguousarray(np.asarray(L0_attn_Wq, dtype=np.float32))
    wk = np.ascontiguousarray(np.asarray(L0_attn_Wk, dtype=np.float32))
    wv = np.ascontiguousarray(np.asarray(L0_attn_Wv, dtype=np.float32))
    wo = np.ascontiguousarray(np.asarray(L0_attn_Wo, dtype=np.float32))
    gw = np.ascontiguousarray(np.asarray(L0_ffn_gate_w, dtype=np.float32))
    uw = np.ascontiguousarray(np.asarray(L0_ffn_up_w, dtype=np.float32))
    dwn = np.ascontiguousarray(np.asarray(L0_ffn_down_w, dtype=np.float32))

    def pcol(v, n):
        return np.ascontiguousarray(np.asarray(v, dtype=np.float32).reshape(n, 128).T)

    bq = pcol(L0_attn_bq, 16)
    bk = pcol(L0_attn_bk, 16)
    bv = pcol(L0_attn_bv, 16)
    bo = pcol(L0_attn_bo, 16)
    gs = pcol(L0_ffn_gate_s, 44)
    us = pcol(L0_ffn_up_s, 44)
    ds = pcol(L0_ffn_down_s, 16)
    ident = np.eye(128, dtype=np.float32)

    nc = _get_nc()
    in_maps = []
    for c in range(NCORES):
        b, h = c // 2, c % 2
        in_maps.append({
            "x": np.ascontiguousarray(X[b, h * T:(h + 1) * T]),
            "wq": wq, "wk": wk, "wv": wv, "wo": wo,
            "gw": gw, "uw": uw, "dw": dwn,
            "bq": bq, "bk": bk, "bv": bv, "bo": bo,
            "gs": gs, "us": us, "ds": ds,
            "sendm": np.full((128, 16), 1.0 if h == 0 else 0.0, np.float32),
            "usem": np.full((128, 16), 1.0 if h == 1 else 0.0, np.float32),
            "ident": ident,
        })
    res = run_bass_kernel_spmd(nc, in_maps, core_ids=list(range(NCORES)))
    out = np.empty((4, 2048, D), np.float32)
    for c in range(NCORES):
        b, h = c // 2, c % 2
        out[b, h * T:(h + 1) * T] = res.results[c]["y"]
    return out


# version 10
# speedup vs baseline: 1.5076x; 1.0089x over previous
import contextlib
import numpy as np

import concourse.bass as bass
import concourse.mybir as mybir
from concourse.bass_utils import run_bass_kernel_spmd

F32 = mybir.dt.float32
BF16 = mybir.dt.bfloat16
F16 = mybir.dt.float16
AF = mybir.ActivationFunctionType
ALU = mybir.AluOpType

D = 2048
T = 1024
ECN = 16
TW = 512
TCN = T // TW
INTER = 5632
ICN = INTER // 128
QN = 4
ICQ = ICN // QN
SCALE = float(128.0 ** -0.5)
NCORES = 8
EPS = 1e-5

ENGINES = ("sync", "scalar", "vector", "tensor", "gpsimd")


class Prog:

    def __init__(self, nc, es, n_dma_sems=24):
        self.nc = nc
        self.ops = {e: [] for e in ENGINES}
        self.esem = {e: es.enter_context(nc.semaphore("es_" + e)) for e in ENGINES}
        self.ecount = {e: 0 for e in ENGINES}
        self.dsems = [es.enter_context(nc.semaphore("dm%d" % i)) for i in range(n_dma_sems)]
        self.duse = [0] * n_dma_sems
        self.dnext = 0
        self.gsems = [es.enter_context(nc.semaphore("gm%d" % i)) for i in range(10)]
        self.guse = [0] * 10
        self.gnext = 0
        self.out_sem = es.enter_context(nc.semaphore("outs"))
        self.out_uses = 0
        self.last_w = {}
        self.readers = {}
        self.waited = {e: {} for e in ENGINES}
        self.seen_keys = set()
        self.graveyard = []
        self.n_inst = 0

    def _marks_for(self, reads, writes):
        marks = []
        for r in reads:
            m = self.last_w.get(r)
            if m is not None:
                marks.append(m)
        for w in writes:
            m = self.last_w.get(w)
            if m is not None:
                marks.append(m)
            for m2 in self.readers.get(w, {}).values():
                marks.append(m2)
            if w not in self.seen_keys:
                marks.extend(self.graveyard)
        return marks

    def _dedupe(self, eng, marks):
        out = []
        w = self.waited[eng]
        for (sem, val) in marks:
            k = id(sem)
            if w.get(k, 0) >= val:
                continue
            w[k] = val
            out.append((sem, val))
        return out

    def _commit(self, eng, mark, reads, writes):
        for wkey in writes:
            self.last_w[wkey] = mark
            self.readers[wkey] = {}
            self.seen_keys.add(wkey)
        for r in reads:
            d = self.readers.setdefault(r, {})
            k = id(mark[0])
            if k not in d or d[k][1] < mark[1]:
                d[k] = mark

    def close_scope(self, keys):
        for k in keys:
            m = self.last_w.get(k)
            if m is not None:
                self.graveyard.append(m)
            for m2 in self.readers.get(k, {}).values():
                self.graveyard.append(m2)
        best = {}
        for (sem, val) in self.graveyard:
            if id(sem) not in best or best[id(sem)][1] < val:
                best[id(sem)] = (sem, val)
        self.graveyard = list(best.values())

    def op(self, eng, fn, reads=(), writes=()):
        return self.macro(eng, [fn], reads=reads, writes=writes)

    def macro(self, eng, fns, reads=(), writes=()):
        reads = list(reads)
        writes = list(writes)
        waits = self._dedupe(eng, self._marks_for(reads, writes))
        self.ecount[eng] += 1
        mark = (self.esem[eng], self.ecount[eng])
        sem = self.esem[eng]
        self.n_inst += len(fns) + len(waits)

        def emit(E, fns=fns, waits=waits, sem=sem):
            for (s, v) in waits:
                E.wait_ge(s, v)
            inst = None
            for f in fns:
                inst = f(E)
            inst.then_inc(sem, 1)
        self.ops[eng].append(emit)
        self._commit(eng, mark, reads, writes)
        return mark

    def raw(self, eng, fn):
        self.ops[eng].append(fn)
        self.n_inst += 1

    def dma(self, eng, fn, reads=(), writes=()):
        reads = list(reads)
        writes = list(writes)
        if eng == "gpsimd":
            j = self.gnext
            self.gnext = (self.gnext + 1) % len(self.gsems)
            s = self.gsems[j]
            use = self.guse
        else:
            j = self.dnext
            self.dnext = (self.dnext + 1) % len(self.dsems)
            s = self.dsems[j]
            use = self.duse
        marks = self._marks_for(reads, writes)
        if use[j] > 0:
            marks.append((s, 16 * use[j]))
        waits = self._dedupe(eng, marks)
        use[j] += 1
        mark = (s, 16 * use[j])
        self.n_inst += 1 + len(waits)

        def emit(E, fn=fn, waits=waits, s=s):
            for (ss, v) in waits:
                E.wait_ge(ss, v)
            fn(E).then_inc(s, 16)
        self.ops[eng].append(emit)
        self._commit(eng, mark, reads, writes)
        return mark

    def dma_out(self, eng, fn, reads=()):
        reads = list(reads)
        waits = self._dedupe(eng, self._marks_for(reads, []))
        self.out_uses += 1
        mark = (self.out_sem, 16 * self.out_uses)
        self.n_inst += 1 + len(waits)

        def emit(E, fn=fn, waits=waits, s=self.out_sem):
            for (ss, v) in waits:
                E.wait_ge(ss, v)
            fn(E).then_inc(s, 16)
        self.ops[eng].append(emit)
        for r in reads:
            d = self.readers.setdefault(r, {})
            k = id(mark[0])
            if k not in d or d[k][1] < mark[1]:
                d[k] = mark
        return mark


class TS:

    def __init__(self, name, dram, row0, col0, width):
        self.name = name
        self.dram = dram
        self.row0 = row0
        self.col0 = col0
        self.width = width


def make_stream(p, demands, pool, pool_tag, wcast, wcast_tag, pf=2):
    info = {}
    state = {"emitted": 0, "lc": 0}

    def emit_one(i):
        ts = demands[i]
        sl = i % len(pool)
        lw = ts.width
        wc = state["lc"] % len(wcast)
        state["lc"] += 1
        kcast = ("wcast", wcast_tag, wc)
        kT = ("wT", pool_tag, sl)
        src = ts.dram[ts.row0:ts.row0 + 128, ts.col0: ts.col0 + lw]
        p.dma("gpsimd",
              lambda E, d=wcast[wc], s=src, lw=lw:
              E.dma_start(out=d[:, 0:lw], in_=s),
              reads=[], writes=[kcast])
        out_ap = (pool[sl][:, 0:lw]
                  .rearrange("q (c e) -> q c e", c=lw // 128))
        p.dma("sync",
              lambda E, o=out_ap, ii=wcast[wc], lw=lw:
              E.dma_start_transpose(out=o, in_=ii[:, 0:lw]),
              reads=[kcast], writes=[kT])
        info[i] = (sl, [kT])

    def get(i):
        hi = min(i + pf, len(demands) - 1)
        while state["emitted"] <= hi:
            emit_one(state["emitted"])
            state["emitted"] += 1
        return info[i]

    return get


def build_nc():
    nc = bass.Bass("TRN2", target_bir_lowering=False)

    dx = nc.dram_tensor("x", [T, D], F32, kind="ExternalInput")
    dwq = nc.dram_tensor("wq", [D, D], F32, kind="ExternalInput")
    dwk = nc.dram_tensor("wk", [D, D], F32, kind="ExternalInput")
    dwv = nc.dram_tensor("wv", [D, D], F32, kind="ExternalInput")
    dwo = nc.dram_tensor("wo", [D, D], F32, kind="ExternalInput")
    dgw = nc.dram_tensor("gw", [INTER, D], F32, kind="ExternalInput")
    duw = nc.dram_tensor("uw", [INTER, D], F32, kind="ExternalInput")
    ddw = nc.dram_tensor("dw", [D, INTER], F32, kind="ExternalInput")
    dbq = nc.dram_tensor("bq", [128, 16], F32, kind="ExternalInput")
    dbk = nc.dram_tensor("bk", [128, 16], F32, kind="ExternalInput")
    dbv = nc.dram_tensor("bv", [128, 16], F32, kind="ExternalInput")
    dbo = nc.dram_tensor("bo", [128, 16], F32, kind="ExternalInput")
    dgs = nc.dram_tensor("gs", [128, 44], F32, kind="ExternalInput")
    dus = nc.dram_tensor("us", [128, 44], F32, kind="ExternalInput")
    dds = nc.dram_tensor("ds", [128, 16], F32, kind="ExternalInput")
    dsendm = nc.dram_tensor("sendm", [128, 16], F32, kind="ExternalInput")
    dusem = nc.dram_tensor("usem", [128, 16], F32, kind="ExternalInput")
    dident = nc.dram_tensor("ident", [128, 128], F32, kind="ExternalInput")
    dy = nc.dram_tensor("y", [T, D], F32, kind="ExternalOutput")
    dccin = nc.dram_tensor("ccin", [128, 16], F32)
    dccout = nc.dram_tensor("ccout", [128, 16], F32)

    es = contextlib.ExitStack()
    with es:
        p = Prog(nc, es)
        sb = nc.sbuf_tensor

        def alive(guard):
            return guard.__enter__(), guard

        s_bq = es.enter_context(sb("s_bq", [128, 16], F32))
        s_bk = es.enter_context(sb("s_bk", [128, 16], F32))
        s_bv = es.enter_context(sb("s_bv", [128, 16], F32))
        s_bo = es.enter_context(sb("s_bo", [128, 16], F32))
        s_gs = es.enter_context(sb("s_gs", [128, 44], F32))
        s_us = es.enter_context(sb("s_us", [128, 44], F32))
        s_ds = es.enter_context(sb("s_ds", [128, 16], F32))
        s_sendm = es.enter_context(sb("s_sendm", [128, 16], F32))
        s_usem = es.enter_context(sb("s_usem", [128, 16], F32))
        s_ls = es.enter_context(sb("s_ls", [128, 16], F32))
        s_ls2 = es.enter_context(sb("s_ls2", [128, 2], F32))
        s_send = es.enter_context(sb("s_send", [128, 16], F32))
        s_crecv = es.enter_context(sb("s_crecv", [128, 16], F32))
        s_carry = es.enter_context(sb("s_carry", [128, 16], F32))
        s_stat = es.enter_context(sb("s_stat", [128, 8], F32))
        s_eps = es.enter_context(sb("s_eps", [128, 1], F32))
        s_ones = es.enter_context(sb("s_ones", [128, T], F32))
        s_xn = es.enter_context(sb("s_xn", [128, ECN * T], BF16))
        s_ctx, g_ctx = alive(sb("ctx", [128, ECN * T], F16))

        pp = [es.enter_context(nc.psum_tensor("pp%d" % i, [128, TW], F32))
              for i in range(6)]
        pt = [es.enter_context(nc.psum_tensor("pt%d" % i, [128, TW], F32))
              for i in range(2)]

        ppi = [0]

        def next_pp():
            i = ppi[0] % 6
            ppi[0] += 1
            return i

        def mm_group(pslot, pairs, reads):
            fns = []
            n = len(pairs)
            for k, (st, mv) in enumerate(pairs):
                fns.append(lambda E, st=st, mv=mv, s=(k == 0), e=(k == n - 1):
                           E.matmul(pp[pslot][:], st, mv, start=s, stop=e))
            return p.macro("tensor", fns, reads=reads, writes=[("pp", pslot)])

        for key, dsrc, sdst in (
            (("bq",), dbq, s_bq), (("bk",), dbk, s_bk), (("bv",), dbv, s_bv),
            (("bo",), dbo, s_bo), (("gs",), dgs, s_gs), (("us",), dus, s_us),
            (("ds",), dds, s_ds), (("sendm",), dsendm, s_sendm),
            (("usem",), dusem, s_usem),
        ):
            p.dma("sync", lambda E, d=sdst, s=dsrc: E.dma_start(out=d[:], in_=s[:, :]),
                  reads=[], writes=[key])
        p.op("gpsimd", lambda E: E.memset(s_ones[:], 1.0), reads=[], writes=[("ones",)])
        p.op("gpsimd", lambda E: E.memset(s_eps[:], EPS), reads=[], writes=[("eps",)])

        s_wcast, g_wcast = zip(*[alive(sb("wcast%d" % i, [128, 2048], BF16)) for i in range(3)])
        s_wT, g_wT = zip(*[alive(sb("wTa%d" % i, [128, 2048], BF16)) for i in range(5)])
        s_t1, g_t1 = alive(sb("t1", [128, TW], F32))
        s_t2, g_t2 = alive(sb("t2", [128, TW], F32))
        s_t3, g_t3 = alive(sb("t3", [128, TW], F32))
        s_kvt, g_kvt = alive(sb("kvt", [128, T], F32))

        s_xl, g_xl = zip(*[alive(sb("xl%d" % i, [128, D], F32)) for i in range(2)])
        s_xnt, g_xnt = zip(*[alive(sb("xnt%d" % i, [128, D], BF16)) for i in range(2)])
        s_sq, g_sq = alive(sb("sq", [128, D], F32))

        for tt in range(8):
            sl = tt % 2
            kx = ("xl", sl)
            p.dma("sync", lambda E, d=s_xl[sl], tt=tt:
                  E.dma_start(out=d[:], in_=dx[tt * 128:(tt + 1) * 128, :]),
                  reads=[], writes=[kx])
            p.op("vector", lambda E, ii=s_xl[sl]:
                 E.reduce_sum(s_stat[:, 0:1], ii[:], axis=mybir.AxisListType.X),
                 reads=[kx], writes=[("st", 0)])
            p.op("scalar", lambda E, ii=s_xl[sl]:
                 E.activation(s_sq[:], ii[:], AF.Square, accum_out=s_stat[:, 1:2]),
                 reads=[kx], writes=[("sq",), ("st", 1)])
            p.op("vector", lambda E:
                 E.tensor_scalar_mul(s_stat[:, 2:3], s_stat[:, 0:1], -1.0 / D),
                 reads=[("st", 0)], writes=[("st", 2)])
            p.op("vector", lambda E:
                 E.tensor_mul(s_stat[:, 3:4], s_stat[:, 2:3], s_stat[:, 2:3]),
                 reads=[("st", 2)], writes=[("st", 3)])
            p.op("vector", lambda E:
                 E.tensor_scalar_mul(s_stat[:, 4:5], s_stat[:, 1:2], 1.0 / D),
                 reads=[("st", 1)], writes=[("st", 4)])
            p.op("vector", lambda E:
                 E.tensor_sub(s_stat[:, 5:6], s_stat[:, 4:5], s_stat[:, 3:4]),
                 reads=[("st", 3), ("st", 4)], writes=[("st", 5)])
            p.op("scalar", lambda E:
                 E.activation(s_stat[:, 6:7], s_stat[:, 5:6], AF.Sqrt, bias=s_eps[:]),
                 reads=[("st", 5), ("eps",)], writes=[("st", 6)])
            p.op("vector", lambda E:
                 E.reciprocal(s_stat[:, 7:8], s_stat[:, 6:7]),
                 reads=[("st", 6)], writes=[("st", 7)])
            kxn = ("xnt", sl)
            p.op("vector", lambda E, ii=s_xl[sl], o=s_xnt[sl]:
                 E.tensor_scalar(o[:], ii[:], s_stat[:, 2:3], s_stat[:, 7:8],
                                 ALU.add, ALU.mult),
                 reads=[kx, ("st", 2), ("st", 7)], writes=[kxn])
            out_ap = (s_xn[:].rearrange("q (c t) -> q c t", c=ECN)
                      [:, :, tt * 128:(tt + 1) * 128])
            p.dma("scalar", lambda E, o=out_ap, ii=s_xnt[sl]:
                  E.dma_start_transpose(out=o, in_=ii[:]),
                  reads=[kxn], writes=[("xn", tt)])

        p.close_scope([("xl", 0), ("xl", 1), ("xnt", 0), ("xnt", 1), ("sq",)])
        g_sq.__exit__(None, None, None)
        for g in reversed(list(g_xnt)):
            g.__exit__(None, None, None)
        for g in reversed(list(g_xl)):
            g.__exit__(None, None, None)

        def xn_reads(tc):
            return [("xn", tt) for tt in range(tc * 4, tc * 4 + 4)]

        def mov_xn(dc, tc):
            return s_xn[:, dc * T + tc * TW: dc * T + tc * TW + TW]

        s_pkv, g_pkv = alive(sb("pkv", [128, ECN * T], F32))
        s_phiqf, g_phiqf = alive(sb("phiqf", [128, 8 * T], F32))

        def phi_evac(pslot, bias, out_ap, out_key):
            kp = ("pp", pslot)
            p.op("vector", lambda E, b=bias, pslot=pslot:
                 E.tensor_scalar(s_t1[:], pp[pslot][:], b, 0.0, ALU.add, ALU.min),
                 reads=[kp], writes=[("t1",)])
            p.op("scalar", lambda E:
                 E.activation(s_t2[:], s_t1[:], AF.Exp),
                 reads=[("t1",)], writes=[("t2",)])
            p.op("scalar", lambda E, b=bias, pslot=pslot:
                 E.activation(s_t3[:], pp[pslot][:], AF.Relu, bias=b),
                 reads=[kp], writes=[("t3",)])
            p.op("vector", lambda E, o=out_ap:
                 E.tensor_add(o, s_t3[:], s_t2[:]),
                 reads=[("t2",), ("t3",)], writes=[out_key])

        kv_demands = []
        for ec in range(ECN):
            kv_demands.append(TS(("K", ec), dwk, ec * 128, 0, D))
            kv_demands.append(TS(("V", ec), dwv, ec * 128, 0, D))
        kv_get = make_stream(p, kv_demands, s_wT, "a", s_wcast, "a", pf=3)

        for ec in range(ECN):
            slk, rk = kv_get(2 * ec)
            slv, rv = kv_get(2 * ec + 1)
            for tc in range(TCN):
                ps = next_pp()
                mm_group(ps, [(s_wT[slk][:, dc * 128:(dc + 1) * 128], mov_xn(dc, tc))
                              for dc in range(ECN)], reads=rk + xn_reads(tc))
                phi_evac(ps, s_bk[:, ec:ec + 1], s_t1[:], ("t1",))
                ps2 = next_pp()
                mm_group(ps2, [(s_wT[slv][:, dc * 128:(dc + 1) * 128], mov_xn(dc, tc))
                               for dc in range(ECN)], reads=rv + xn_reads(tc))
                p.op("scalar", lambda E, b=s_bv[:, ec:ec + 1], ps2=ps2:
                     E.activation(s_t3[:], pp[ps2][:], AF.Identity, bias=b),
                     reads=[("pp", ps2)], writes=[("t3",)])
                p.op("vector", lambda E, ec=ec, tc=tc:
                     E.scalar_tensor_tensor(
                         s_pkv[:, ec * T + tc * TW: ec * T + (tc + 1) * TW],
                         s_t1[:], 1.0, s_t3[:], ALU.mult, ALU.mult,
                         accum_out=s_ls2[:, tc:tc + 1]),
                     reads=[("t1",), ("t3",)],
                     writes=[("pkv", ec, tc), ("ls2", tc)])
            p.op("vector", lambda E, ec=ec:
                 E.tensor_add(s_ls[:, ec:ec + 1], s_ls2[:, 0:1], s_ls2[:, 1:2]),
                 reads=[("ls2", 0), ("ls2", 1)], writes=[("ls", ec)])

        p.op("vector", lambda E:
             E.tensor_mul(s_send[:], s_ls[:], s_sendm[:]),
             reads=[("sendm",)] + [("ls", ec) for ec in range(ECN)],
             writes=[("send",)])

        q_demands = [TS(("Q", ec), dwq, ec * 128, 0, D) for ec in range(ECN)]
        q_get = make_stream(p, q_demands, s_wT, "a", s_wcast, "a", pf=3)
        q_get(0)
        q_get(1)

        p.dma("gpsimd", lambda E: E.dma_start(out=dccin[:, :], in_=s_send[:]),
              reads=[("send",)], writes=[("ccin",)])
        p.op("gpsimd", lambda E:
             E.collective_compute(
                 "AllReduce", ALU.add,
                 replica_groups=[[0, 1], [2, 3], [4, 5], [6, 7]],
                 ins=[dccin.ap().opt()], outs=[dccout.ap().opt()]),
             reads=[("ccin",)], writes=[("ccout",)])
        p.dma("gpsimd", lambda E: E.dma_start(out=s_crecv[:], in_=dccout[:, :]),
              reads=[("ccout",)], writes=[("crecv",)])
        p.op("gpsimd", lambda E:
             E.tensor_mul(s_carry[:], s_crecv[:], s_usem[:]),
             reads=[("crecv",), ("usem",)], writes=[("carry",)])
        def emit_pair(ec):
            e8 = ec % 8
            p.op("vector", lambda E, ec=ec:
                 E.tensor_tensor_scan(
                     s_kvt[:], s_ones[:],
                     s_pkv[:, ec * T:(ec + 1) * T], s_carry[:, ec:ec + 1],
                     ALU.mult, ALU.add),
                 reads=[("ones",), ("carry",), ("pkv", ec, 0), ("pkv", ec, 1)],
                 writes=[("kvt",)])
            p.op("vector", lambda E, ec=ec, e8=e8:
                 E.scalar_tensor_tensor(
                     s_ctx[:, ec * T:(ec + 1) * T],
                     s_phiqf[:, e8 * T:(e8 + 1) * T], SCALE, s_kvt[:],
                     ALU.mult, ALU.mult),
                 reads=[("kvt",), ("phiqf", e8, 0), ("phiqf", e8, 1)],
                 writes=[("ctx", ec)])

        pair_sched = {6: [0], 7: [1], 8: [2], 9: [3], 10: [4, 8], 11: [5, 9],
                      12: [6, 10], 13: [7, 11], 14: [12, 13], 15: [14]}
        for it in range(16):
            e8 = it % 8
            ec = it
            sl, rk = q_get(ec)
            for tc in range(TCN):
                ps = next_pp()
                mm_group(ps, [(s_wT[sl][:, dc * 128:(dc + 1) * 128], mov_xn(dc, tc))
                              for dc in range(ECN)], reads=rk + xn_reads(tc))
                phi_evac(ps, s_bq[:, ec:ec + 1],
                         s_phiqf[:, e8 * T + tc * TW: e8 * T + (tc + 1) * TW],
                         ("phiqf", e8, tc))
            for pec in pair_sched.get(it, []):
                emit_pair(pec)
        emit_pair(15)

        p.close_scope(
            [("pkv", ec, tc) for ec in range(ECN) for tc in range(TCN)]
            + [("kvt",), ("t1",), ("t2",), ("t3",)]
            + [("phiqf", e8, tc) for e8 in range(8) for tc in range(TCN)]
            + [("wT", "a", sl) for sl in range(5)]
            + [("wcast", "a", i) for i in range(3)])
        g_phiqf.__exit__(None, None, None)
        g_pkv.__exit__(None, None, None)
        g_kvt.__exit__(None, None, None)
        g_t3.__exit__(None, None, None)
        g_t2.__exit__(None, None, None)
        g_t1.__exit__(None, None, None)
        for g in reversed(list(g_wT)):
            g.__exit__(None, None, None)
        for g in reversed(list(g_wcast)):
            g.__exit__(None, None, None)

        s_acc, g_acc = alive(sb("s_acc", [128, ECN * T], F32))
        s_ocast, g_ocast = zip(*[alive(sb("ocast%d" % i, [128, 2048], F16)) for i in range(3)])
        s_owT, g_owT = zip(*[alive(sb("owT%d" % i, [128, 2048], F16)) for i in range(3)])

        o_demands = [TS(("O", ec), dwo, ec * 128, 0, D) for ec in range(ECN)]
        o_get = make_stream(p, o_demands, s_owT, "o", s_ocast, "o", pf=2)

        def mov_ctx(dc, tc):
            return s_ctx[:, dc * T + tc * TW: dc * T + tc * TW + TW]

        ctx_reads = [("ctx", ec) for ec in range(ECN)]
        for ec in range(ECN):
            sl, rk = o_get(ec)
            for tc in range(TCN):
                ps = next_pp()
                mm_group(ps, [(s_owT[sl][:, dc * 128:(dc + 1) * 128], mov_ctx(dc, tc))
                              for dc in range(ECN)], reads=rk + ctx_reads)
                p.op("vector", lambda E, ec=ec, tc=tc, ps=ps:
                     E.tensor_scalar(
                         s_acc[:, ec * T + tc * TW: ec * T + (tc + 1) * TW],
                         pp[ps][:], s_bo[:, ec:ec + 1], None, ALU.add),
                     reads=[("pp", ps), ("bo",)], writes=[("acc", ec, tc)])

        p.close_scope(
            [("wT", "o", sl) for sl in range(3)]
            + [("wcast", "o", i) for i in range(3)])
        for g in reversed(list(g_owT)):
            g.__exit__(None, None, None)
        for g in reversed(list(g_ocast)):
            g.__exit__(None, None, None)

        s_hdn, g_hdn = alive(sb("hdn", [128, ICQ * T], BF16))
        s_fc, g_fc = zip(*[alive(sb("fc%d" % i, [128, 2048], BF16)) for i in range(3)])
        s_gu, g_gu = zip(*[alive(sb("gu%d" % i, [128, 2048], BF16)) for i in range(5)])
        s_dwT, g_dwT = zip(*[alive(sb("dwT%d" % i, [128, 1408], BF16)) for i in range(2)])
        s_sg, g_sg = alive(sb("sgt", [128, TW], F32))
        s_su, g_su = alive(sb("sut", [128, TW], F32))

        gu_demands = []
        for q in range(QN):
            for il in range(ICQ):
                ic = q * ICQ + il
                gu_demands.append(TS(("G", ic), dgw, ic * 128, 0, D))
                gu_demands.append(TS(("U", ic), duw, ic * 128, 0, D))
        gu_get = make_stream(p, gu_demands, s_gu, "gu", s_fc, "f", pf=3)
        dn_all = [TS(("Dn", hc, q), ddw, hc * 128, q * 1408, 1408)
                  for q in range(QN) for hc in range(ECN)]
        dn_get = make_stream(p, dn_all, s_dwT, "dn", s_fc, "f", pf=1)

        for q in range(QN):
            for il in range(ICQ):
                ic = q * ICQ + il
                slg, rg = gu_get(2 * ic)
                slu, ru = gu_get(2 * ic + 1)
                for tc in range(TCN):
                    ps = next_pp()
                    mm_group(ps, [(s_gu[slg][:, dc * 128:(dc + 1) * 128], mov_xn(dc, tc))
                                  for dc in range(ECN)], reads=rg + xn_reads(tc))
                    p.op("scalar", lambda E, ic=ic, ps=ps:
                         E.activation(s_sg[:], pp[ps][:], AF.Silu,
                                      scale=s_gs[:, ic:ic + 1]),
                         reads=[("pp", ps), ("gs",)], writes=[("sg",)])
                    ps2 = next_pp()
                    mm_group(ps2, [(s_gu[slu][:, dc * 128:(dc + 1) * 128], mov_xn(dc, tc))
                                   for dc in range(ECN)], reads=ru + xn_reads(tc))
                    p.op("scalar", lambda E, ic=ic, ps2=ps2:
                         E.activation(s_su[:], pp[ps2][:], AF.Identity,
                                      scale=s_us[:, ic:ic + 1]),
                         reads=[("pp", ps2), ("us",)], writes=[("su",)])
                    p.op("vector", lambda E, il=il, tc=tc:
                         E.tensor_mul(
                             s_hdn[:, il * T + tc * TW: il * T + (tc + 1) * TW],
                             s_sg[:], s_su[:]),
                         reads=[("sg",), ("su",)], writes=[("hdn", il, tc)])

            hdn_reads = [("hdn", il, tc) for il in range(ICQ) for tc in range(TCN)]
            for hc in range(ECN):
                sl, rk = dn_get(q * ECN + hc)
                for tc in range(TCN):
                    ps = next_pp()
                    mm_group(ps, [(s_dwT[sl][:, il * 128:(il + 1) * 128],
                                   s_hdn[:, il * T + tc * TW: il * T + tc * TW + TW])
                                  for il in range(ICQ)], reads=rk + hdn_reads)
                    p.op("vector", lambda E, hc=hc, tc=tc, ps=ps:
                         E.scalar_tensor_tensor(
                             s_acc[:, hc * T + tc * TW: hc * T + (tc + 1) * TW],
                             pp[ps][:], s_ds[:, hc:hc + 1],
                             s_acc[:, hc * T + tc * TW: hc * T + (tc + 1) * TW],
                             ALU.mult, ALU.add),
                         reads=[("pp", ps), ("ds",), ("acc", hc, tc)],
                         writes=[("acc", hc, tc)])

        ffn_keys = ([("hdn", il, tc) for il in range(ICQ) for tc in range(TCN)]
                    + [("sg",), ("su",)]
                    + [("wcast", "f", i) for i in range(3)])
        for q in range(QN):
            ffn_keys += [("wT", ("gu", q), sl) for sl in range(5)]
            ffn_keys += [("wT", ("dn", q), sl) for sl in range(2)]
        p.close_scope(ffn_keys)
        g_su.__exit__(None, None, None)
        g_sg.__exit__(None, None, None)
        for g in reversed(list(g_dwT)):
            g.__exit__(None, None, None)
        for g in reversed(list(g_gu)):
            g.__exit__(None, None, None)
        for g in reversed(list(g_fc)):
            g.__exit__(None, None, None)
        g_hdn.__exit__(None, None, None)

        s_xt, g_xt = zip(*[alive(sb("xt%d" % i, [128, D], F32)) for i in range(3)])
        s_yt, g_yt = zip(*[alive(sb("yt%d" % i, [128, D], F32)) for i in range(3)])
        s_id, g_id = alive(sb("identity", [128, 128], F32))
        p.dma("sync", lambda E: E.dma_start(out=s_id[:], in_=dident[:, :]),
              reads=[], writes=[("id",)])
        out_marks = []

        for tt in range(8):
            sl = tt % 3
            kx = ("xt", sl)
            p.dma("sync", lambda E, d=s_xt[sl], tt=tt:
                  E.dma_start(out=d[:], in_=dx[tt * 128:(tt + 1) * 128, :]),
                  reads=[], writes=[kx])
            tc = tt // 4
            ptpool = [pt[0], pt[1], pp[4], pp[5]]
            for dc in range(ECN):
                pslot = dc % 4
                extra = [("pp", pslot + 2)] if pslot >= 2 else []
                p.op("tensor", lambda E, dc=dc, tt=tt, pslot=pslot, ptpool=ptpool:
                     E.transpose(ptpool[pslot][:, 0:128],
                                 s_acc[:, dc * T + tt * 128: dc * T + (tt + 1) * 128],
                                 s_id[:]),
                     reads=[("id",), ("acc", dc, tc)], writes=[("pt", pslot)] + extra)
                p.op("vector", lambda E, dc=dc, sl=sl, pslot=pslot, ptpool=ptpool:
                     E.tensor_add(s_yt[sl][:, dc * 128:(dc + 1) * 128],
                                  s_xt[sl][:, dc * 128:(dc + 1) * 128],
                                  ptpool[pslot][:, 0:128]),
                     reads=[kx, ("pt", pslot)], writes=[("yt", sl, dc)])
            out_marks.append(p.dma("scalar", lambda E, sl=sl, tt=tt:
                      E.dma_start(out=dy[tt * 128:(tt + 1) * 128, :], in_=s_yt[sl][:]),
                      reads=[("yt", sl, dc) for dc in range(ECN)]))

        final = {}
        for (sem, val) in out_marks:
            if id(sem) not in final or final[id(sem)][1] < val:
                final[id(sem)] = (sem, val)
        for (sem, val) in final.values():
            p.raw("sync", lambda E, sem=sem, val=val: E.wait_ge(sem, val))

        g_id.__exit__(None, None, None)
        for g in reversed(list(g_yt)):
            g.__exit__(None, None, None)
        for g in reversed(list(g_xt)):
            g.__exit__(None, None, None)
        g_acc.__exit__(None, None, None)
        g_ctx.__exit__(None, None, None)

        with nc.Block() as block:
            block.sync(lambda E: [f(E) for f in p.ops["sync"]])
            block.scalar(lambda E: [f(E) for f in p.ops["scalar"]])
            block.vector(lambda E: [f(E) for f in p.ops["vector"]])
            block.tensor(lambda E: [f(E) for f in p.ops["tensor"]])
            block.gpsimd(lambda E: [f(E) for f in p.ops["gpsimd"]])

    return nc


_NC = None


def _get_nc():
    global _NC
    if _NC is None:
        _NC = build_nc()
    return _NC


def kernel(L0_attn_x, L0_attn_Wq, L0_attn_bq, L0_attn_Wk, L0_attn_bk,
           L0_attn_Wv, L0_attn_bv, L0_attn_Wo, L0_attn_bo,
           L0_ffn_gate_w, L0_ffn_up_w, L0_ffn_down_w,
           L0_ffn_gate_s, L0_ffn_up_s, L0_ffn_down_s):
    X = np.ascontiguousarray(np.asarray(L0_attn_x, dtype=np.float32))
    wq = np.ascontiguousarray(np.asarray(L0_attn_Wq, dtype=np.float32))
    wk = np.ascontiguousarray(np.asarray(L0_attn_Wk, dtype=np.float32))
    wv = np.ascontiguousarray(np.asarray(L0_attn_Wv, dtype=np.float32))
    wo = np.ascontiguousarray(np.asarray(L0_attn_Wo, dtype=np.float32))
    gw = np.ascontiguousarray(np.asarray(L0_ffn_gate_w, dtype=np.float32))
    uw = np.ascontiguousarray(np.asarray(L0_ffn_up_w, dtype=np.float32))
    dwn = np.ascontiguousarray(np.asarray(L0_ffn_down_w, dtype=np.float32))

    def pcol(v, n):
        return np.ascontiguousarray(np.asarray(v, dtype=np.float32).reshape(n, 128).T)

    bq = pcol(L0_attn_bq, 16)
    bk = pcol(L0_attn_bk, 16)
    bv = pcol(L0_attn_bv, 16)
    bo = pcol(L0_attn_bo, 16)
    gs = pcol(L0_ffn_gate_s, 44)
    us = pcol(L0_ffn_up_s, 44)
    ds = pcol(L0_ffn_down_s, 16)
    ident = np.eye(128, dtype=np.float32)

    nc = _get_nc()
    in_maps = []
    for c in range(NCORES):
        b, h = c // 2, c % 2
        in_maps.append({
            "x": np.ascontiguousarray(X[b, h * T:(h + 1) * T]),
            "wq": wq, "wk": wk, "wv": wv, "wo": wo,
            "gw": gw, "uw": uw, "dw": dwn,
            "bq": bq, "bk": bk, "bv": bv, "bo": bo,
            "gs": gs, "us": us, "ds": ds,
            "sendm": np.full((128, 16), 1.0 if h == 0 else 0.0, np.float32),
            "usem": np.full((128, 16), 1.0 if h == 1 else 0.0, np.float32),
            "ident": ident,
        })
    res = run_bass_kernel_spmd(nc, in_maps, core_ids=list(range(NCORES)))
    out = np.empty((4, 2048, D), np.float32)
    for c in range(NCORES):
        b, h = c // 2, c % 2
        out[b, h * T:(h + 1) * T] = res.results[c]["y"]
    return out
